# version 12
# baseline (speedup 1.0000x reference)
import numpy as np
import concourse.bass as bass
import concourse.mybir as mybir
from concourse.bass_utils import run_bass_kernel_spmd
from contextlib import ExitStack

F32 = mybir.dt.float32
F16 = mybir.dt.float16
AF = mybir.ActivationFunctionType
ALU = mybir.AluOpType

D = 2048
NDT = 16
NT = 1024
TC = 512
NCH = NT // TC
NF = 8192
NFT = 64
DENOM = float(np.float32(np.sqrt(np.float32(2048.0))) / np.float32(2048.0))
EPS = 1e-5
RECIP_D = float(np.float32(1.0) / np.float32(2048.0))

ENGINES = ("sync", "scalar", "vector", "tensor", "gpsimd")


class Plan:

    def __init__(self, nc, sems):
        self.nc = nc
        self.sems = sems
        self.counts = {k: 0 for k in sems}
        self.items = {e: [] for e in ENGINES}
        self.last_wait = {}

    def emit(self, eng, fn, waits=(), inc=None, inc_amt=1):
        wl = []
        for w in waits:
            if w is None:
                continue
            s, v = w
            if v <= 0:
                continue
            assert v <= self.counts[s], (s, v, self.counts[s])
            key = (eng, s)
            if self.last_wait.get(key, -1) >= v:
                continue
            self.last_wait[key] = v
            wl.append((s, v))
        after = None
        if inc is not None:
            self.counts[inc] += inc_amt
            after = self.counts[inc]
        self.items[eng].append((wl, fn, inc, inc_amt))
        return after

    def now(self, sem):
        return (sem, self.counts[sem])

    def run(self, eng_name, eng):
        for wl, fn, inc, inc_amt in self.items[eng_name]:
            for s, v in wl:
                eng.wait_ge(self.sems[s], v)
            ins = fn(eng)
            if inc is not None:
                ins.then_inc(self.sems[inc], inc_amt)


def build_program(n_cores=8):
    nc = bass.Bass("TRN2", target_bir_lowering=False)

    xT = nc.dram_tensor("xT", [D, NT], F32, kind="ExternalInput")
    wqT = nc.dram_tensor("wqT", [D, D], F32, kind="ExternalInput")
    wkT = nc.dram_tensor("wkT", [D, D], F32, kind="ExternalInput")
    wvT = nc.dram_tensor("wvT", [D, D], F32, kind="ExternalInput")
    woT = nc.dram_tensor("woT", [D, D], F32, kind="ExternalInput")
    w0T = nc.dram_tensor("w0T", [D, NF], F32, kind="ExternalInput")
    w1T = nc.dram_tensor("w1T", [NF, D], F32, kind="ExternalInput")
    b0c = nc.dram_tensor("b0c", [128, NFT], F32, kind="ExternalInput")
    b1c = nc.dram_tensor("b1c", [128, NDT], F32, kind="ExternalInput")
    csel = nc.dram_tensor("csel", [8, 1], F32, kind="ExternalInput")
    ident = nc.dram_tensor("ident", [128, 128], F32, kind="ExternalInput")
    y = nc.dram_tensor("y", [D, NT], F32, kind="ExternalOutput")
    q_page = nc.dram_tensor("q_page", [D, NT], F32, kind="Internal")
    ag_in = nc.dram_tensor("ag_in", [D], F32, kind="Internal")
    ag_out = nc.dram_tensor("ag_out", [n_cores, D], F32, kind="Internal")

    def hbm_tiles(t, ncols=NT):
        return bass.AP(t, 0, [[ncols, 128], [128 * ncols, NDT], [1, ncols]])

    def hbm_tile_slice(t, i, c0, w, ncols=NT):
        return bass.AP(t, i * 128 * ncols + c0, [[ncols, 128], [1, w]])

    def strip(t, row0, col0, tot_cols, ncols=128):
        return bass.AP(t, row0 * tot_cols + col0,
                       [[tot_cols, 128], [128 * tot_cols, NDT], [1, ncols]])

    with ExitStack() as st:
        def sb(stk, name, shape, dt):
            return stk.enter_context(nc.sbuf_tensor(name, shape, dt))

        sems = {}
        for s in ("dS0", "dS1", "dG", "dQ", "dR", "dR1", "dA", "tS", "tV", "tP", "tG"):
            sems[s] = st.enter_context(nc.semaphore(s))
        P = Plan(nc, sems)

        xs = sb(st, "xs", [128, NDT * NT], F32)
        bcA = sb(st, "bcA", [128, NT], F32)
        bcB = sb(st, "bcB", [128, NT], F32)
        b0_sb = sb(st, "b0_sb", [128, NFT], F32)
        b1_sb = sb(st, "b1_sb", [128, NDT], F32)
        csel_sb = sb(st, "csel_sb", [8, 1], F32)
        ident_sb = sb(st, "ident_sb", [128, 128], F32)
        ones_sc = sb(st, "ones_sc", [128, NT], F16)
        onescol = sb(st, "onescol", [1, 128], F32)
        onesrow = sb(st, "onesrow", [128, 1], F32)
        statrow = sb(st, "statrow", [1, 512], F32)
        s1acc = sb(st, "s1acc", [128, NDT * NCH], F32)
        s1T = sb(st, "s1T", [128, NDT], F32)
        s1row = sb(st, "s1row", [NDT, 128], F32)
        carry_sb = sb(st, "carry_sb", [128, NDT], F32)
        sq_scr = sb(st, "sq_scr", [128, NT], F32)

        P0 = st.enter_context(nc.psum_tensor("P0", [128, 512], F32))
        P1 = st.enter_context(nc.psum_tensor("P1", [128, 512], F32))
        P2 = st.enter_context(nc.psum_tensor("P2", [128, 512], F32))
        P3 = st.enter_context(nc.psum_tensor("P3", [128, 512], F32))
        PB = st.enter_context(nc.psum_tensor("PB", [128, 1024], F32))
        PR1 = st.enter_context(nc.psum_tensor("PR1", [1, 512], F32))
        PR2 = st.enter_context(nc.psum_tensor("PR2", [1, 512], F32))

        def xst(i, t0=0, w=NT):
            return xs[:, i * NT + t0: i * NT + t0 + w]

        x_loaded = P.emit("gpsimd", lambda e: e.dma_start(out=xs[:, :], in_=hbm_tiles(xT)),
                          inc="dG", inc_amt=16)
        c1 = P.emit("gpsimd", lambda e: e.dma_start(out=b0_sb[:, :], in_=b0c.ap()),
                    waits=[("dG", x_loaded)], inc="dG", inc_amt=16)
        c2 = P.emit("gpsimd", lambda e: e.dma_start(out=b1_sb[:, :], in_=b1c.ap()),
                    waits=[("dG", c1)], inc="dG", inc_amt=16)
        c3 = P.emit("gpsimd", lambda e: e.dma_start(out=csel_sb[:, :], in_=csel.ap()),
                    waits=[("dG", c2)], inc="dG", inc_amt=16)
        cst_loaded = P.emit("gpsimd", lambda e: e.dma_start(out=ident_sb[:, :], in_=ident.ap()),
                            waits=[("dG", c3)], inc="dG", inc_amt=16)
        m1 = P.emit("vector", lambda e: e.memset(ones_sc[:, :], 1.0), inc="tV")
        m2 = P.emit("vector", lambda e: e.memset(onescol[:, :], 1.0), inc="tV")
        m3 = P.emit("vector", lambda e: e.memset(onesrow[:, :], 1.0), inc="tV")

        state = {"xg": [[("dG", x_loaded)] for _ in range(NDT)],
                 "xg_h": [[[("dG", x_loaded)] for _ in range(NDT)] for _ in range(2)],
                 "bc_read": [0, 0], "ln_prev": {}}

        class LNSteps:

            def __init__(self):
                self.sum_last = 0
                self.sq_last = 0
                self.sq_mm = [0, 0]
                self.sq_op = [0, 0]
                self.scr_guard = 0
                self.fin = {}
                prev = state.get("ln_prev", {})
                self.bmm = prev.get("bmm", 0)
                self.pr_guard = prev.get("pr_guard", [])

            def step(self, i, half):
                xgw = list(state["xg_h"][half][i])
                par = i % 2
                sq = P.emit("scalar",
                            (lambda i_, h, p_: lambda e: e.activation(
                                out=sq_scr[:, p_ * 512:(p_ + 1) * 512],
                                in_=xs[:, i_ * NT + h * 512: i_ * NT + h * 512 + 512],
                                func=AF.Square, bias=0.0, scale=1.0))(i, half, par),
                            waits=xgw + [("tP", self.sq_mm[par]), ("tS", self.sq_op[par]),
                                         ("tV", self.scr_guard)],
                            inc="tS")
                self.sq_op[par] = sq
                first_w = self.pr_guard if i == 0 else []
                self.sum_last = P.emit(
                    "tensor",
                    (lambda i_, h: lambda e: e.matmul(
                        PR1[0:1, 0:512], onesrow[:, :],
                        xs[:, i_ * NT + h * 512: i_ * NT + h * 512 + 512],
                        start=(i_ == 0), stop=(i_ == NDT - 1)))(i, half),
                    waits=xgw + [("tV", m3)] + first_w,
                    inc="tP" if i == NDT - 1 else None)
                self.sq_mm[par] = P.emit(
                    "tensor",
                    (lambda i_, p_: lambda e: e.matmul(
                        PR2[0:1, 0:512], onesrow[:, :],
                        sq_scr[:, p_ * 512:(p_ + 1) * 512],
                        start=(i_ == 0), stop=(i_ == NDT - 1)))(i, par),
                    waits=[("tS", sq)] + first_w, inc="tP")
                self.sq_last = self.sq_mm[par]

            def finish_half(self, h):
                hs = slice(h * 512, (h + 1) * 512)
                bcr = state["bc_read"][h]
                cp1 = P.emit("vector",
                             (lambda: lambda e: e.tensor_copy(
                                 out=statrow[0:1, 0:512], in_=PR1[0:1, 0:512]))(),
                             waits=[("tP", self.sum_last), ("tP", self.bmm)], inc="tV")
                bmm_a = P.emit("tensor",
                               (lambda h_: lambda e: e.matmul(
                                   PB[:, h_ * 512:(h_ + 1) * 512], onescol[:, :],
                                   statrow[0:1, 0:512], start=True, stop=True))(h),
                               waits=[("tV", cp1), ("tV", m2)], inc="tP")
                meanv = P.emit("vector",
                               (lambda h_: lambda e: e.tensor_scalar(
                                   out=bcB[:, h_ * 512:(h_ + 1) * 512],
                                   in0=PB[:, h_ * 512:(h_ + 1) * 512],
                                   scalar1=RECIP_D, scalar2=None, op0=ALU.mult))(h),
                               waits=[("tP", bmm_a), ("tV", bcr)], inc="tV")
                cp2 = P.emit("vector",
                             (lambda: lambda e: e.tensor_copy(
                                 out=statrow[0:1, 0:512], in_=PR2[0:1, 0:512]))(),
                             waits=[("tP", self.sq_last), ("tP", bmm_a)], inc="tV")
                bmm_b = P.emit("tensor",
                               (lambda h_: lambda e: e.matmul(
                                   PB[:, h_ * 512:(h_ + 1) * 512], onescol[:, :],
                                   statrow[0:1, 0:512], start=True, stop=True))(h),
                               waits=[("tV", cp2), ("tV", meanv)], inc="tP")
                t1 = P.emit("vector",
                            (lambda h_: lambda e: e.tensor_scalar(
                                out=sq_scr[:, 0:512], in0=PB[:, h_ * 512:(h_ + 1) * 512],
                                scalar1=RECIP_D, scalar2=None, op0=ALU.mult))(h),
                            waits=[("tP", bmm_b), ("tP", self.sq_last)], inc="tV")
                t2 = P.emit("vector",
                            (lambda h_: lambda e: e.tensor_tensor(
                                out=bcA[:, h_ * 512:(h_ + 1) * 512],
                                in0=bcB[:, h_ * 512:(h_ + 1) * 512],
                                in1=bcB[:, h_ * 512:(h_ + 1) * 512], op=ALU.mult))(h),
                            waits=[("tV", meanv), ("tV", bcr)], inc="tV")
                t3 = P.emit("vector",
                            (lambda h_: lambda e: e.tensor_tensor(
                                out=bcA[:, h_ * 512:(h_ + 1) * 512], in0=sq_scr[:, 0:512],
                                in1=bcA[:, h_ * 512:(h_ + 1) * 512], op=ALU.subtract))(h),
                            waits=[("tV", t1), ("tV", t2)], inc="tV")
                t3e = P.emit("vector",
                             (lambda h_: lambda e: e.tensor_scalar(
                                 out=bcA[:, h_ * 512:(h_ + 1) * 512],
                                 in0=bcA[:, h_ * 512:(h_ + 1) * 512],
                                 scalar1=EPS, scalar2=None, op0=ALU.add))(h),
                             waits=[("tV", t3)], inc="tV")
                t4 = P.emit("scalar",
                            (lambda h_: lambda e: e.activation(
                                out=bcA[:, h_ * 512:(h_ + 1) * 512],
                                in_=bcA[:, h_ * 512:(h_ + 1) * 512],
                                func=AF.Sqrt, bias=0.0, scale=1.0))(h),
                            waits=[("tV", t3e)], inc="tS")
                t4b = P.emit("vector",
                             (lambda h_: lambda e: e.reciprocal(
                                 out=bcA[:, h_ * 512:(h_ + 1) * 512],
                                 in_=bcA[:, h_ * 512:(h_ + 1) * 512]))(h),
                             waits=[("tS", t4)], inc="tV")
                t5 = P.emit("vector",
                            (lambda h_: lambda e: e.scalar_tensor_tensor(
                                out=bcB[:, h_ * 512:(h_ + 1) * 512],
                                in0=bcB[:, h_ * 512:(h_ + 1) * 512], scalar=-1.0,
                                in1=bcA[:, h_ * 512:(h_ + 1) * 512],
                                op0=ALU.mult, op1=ALU.mult))(h),
                            waits=[("tV", t4b)], inc="tV")
                self.scr_guard = t3
                self.bmm = bmm_b
                self.fin[h] = [("tV", t4b), ("tV", t5)]
                if h == 1:
                    state["ln_prev"] = {"bmm": bmm_b,
                                        "pr_guard": [("tV", cp1), ("tV", cp2)]}

        class WSplit:

            def __init__(self, wtmp, whi, wlo):
                self.wtmp = wtmp if isinstance(wtmp, list) else [wtmp]
                self.nw = len(self.wtmp)
                self.whi, self.wlo = whi, wlo
                self.hi = [P.counts["tS"]] * 2
                self.lo = [P.counts["tV"]] * 2
                self.pe = [P.counts["tP"]] * 2
                self.wh = [P.counts["tS"]] * self.nw
                self.wl = [P.counts["tV"]] * self.nw
                self.seq = 0

            def load(self, src_ap, b):
                w = self.seq % self.nw
                self.seq += 1
                sem = ("dS0", "dS1")[w]
                d = P.emit("sync",
                           (lambda ap_, w_: lambda e: e.dma_start(
                               out=self.wtmp[w_][:, :], in_=ap_))(src_ap, w),
                           waits=[("tS", self.wh[w]), ("tV", self.wl[w])],
                           inc=sem, inc_amt=16)
                self.hi[b] = P.emit("scalar",
                                    (lambda b_, w_: lambda e: e.activation(
                                        out=self.whi[b_][:, :], in_=self.wtmp[w_][:, :],
                                        func=AF.Copy, bias=0.0, scale=1.0))(b, w),
                                    waits=[(sem, d), ("tP", self.pe[b])], inc="tS")
                self.lo[b] = P.emit("vector",
                                    (lambda b_, w_: lambda e: e.tensor_tensor(
                                        out=self.wlo[b_][:, :], in0=self.wtmp[w_][:, :],
                                        in1=self.whi[b_][:, :], op=ALU.subtract))(b, w),
                                    waits=[("tS", self.hi[b]), ("tP", self.pe[b])], inc="tV")
                self.wh[w] = self.hi[b]
                self.wl[w] = self.lo[b]
                return [("tS", self.hi[b]), ("tV", self.lo[b])]

        def run_pipe(ws, stages, hooks=None, pend=None):
            pend = dict(pend) if pend else {}
            n = len(stages)
            for idx in range(min(2, n)):
                if idx not in pend:
                    pend[idx] = ws.load(stages[idx][0], idx % 2)
            for idx in range(n):
                b = idx % 2
                _, group, post = stages[idx]
                gdone = group(b, pend[idx])
                ws.pe[b] = gdone
                if idx + 2 < n:
                    pend[idx + 2] = ws.load(stages[idx + 2][0], (idx + 2) % 2)
                if post is not None:
                    post(gdone)
                if hooks and idx in hooks:
                    for fn in hooks[idx]:
                        fn()

        def emit_ffn(tag):
            ph = ExitStack()
            X16 = sb(ph, tag + "_X16", [128, NDT * TC], F16)
            D16 = sb(ph, tag + "_D16", [128, NFT * TC], F16)
            wtmp = [sb(ph, tag + f"_wtmp{w}", [128, 2048], F32) for w in range(2)]
            whi = [sb(ph, tag + f"_whi{b}", [128, 2048], F16) for b in range(2)]
            wlo = [sb(ph, tag + f"_wlo{b}", [128, 2048], F16) for b in range(2)]
            E16 = [sb(ph, tag + f"_E16{b}", [128, TC], F16) for b in range(2)]
            ap_scr = sb(ph, tag + "_apscr", [128, TC], F32)

            ws = WSplit(wtmp, whi, wlo)
            psg = {k: P.now("tV") for k in range(4)}
            e_guard = [P.now("tV"), P.now("tV")]
            ln = LNSteps()

            def emit_apply(tc, xwar):
                guards = []
                done = 0
                for i in range(NDT):
                    a1 = P.emit("vector",
                                (lambda i_, tc_: lambda e: e.tensor_tensor(
                                    out=ap_scr[:, 0:TC], in0=xst(i_, tc_ * TC, TC),
                                    in1=bcA[:, tc_ * TC:(tc_ + 1) * TC], op=ALU.mult))(i, tc),
                                waits=ln.fin[tc] + list(state["xg_h"][tc][i]) + [xwar, P.now("tV")],
                                inc="tV")
                    done = P.emit("vector",
                                  (lambda i_, tc_: lambda e: e.tensor_tensor(
                                      out=X16[:, i_ * TC:(i_ + 1) * TC], in0=ap_scr[:, 0:TC],
                                      in1=bcB[:, tc_ * TC:(tc_ + 1) * TC], op=ALU.add))(i, tc),
                                  waits=[("tV", a1)], inc="tV")
                    guards.append(("tV", done))
                state["bc_read"][tc] = done
                return guards

            for i in range(NDT):
                ln.step(i, 0)
            ln.finish_half(0)
            mm1_pend = {0: ws.load(strip(w0T, 0, 0, NF), 0),
                        1: ws.load(strip(w0T, 0, 128, NF), 1)}
            x16_g = emit_apply(0, ("tP", 0))
            for i in range(NDT):
                ln.step(i, 1)
            ln.finish_half(1)
            for tc in range(NCH):
                d16_g = {}

                def mk_mm1(j, tc_, x16g_):
                    psid = j % 2
                    ps = (P0, P1)[psid]
                    perk = (j == 0)

                    def group(b, wready):
                        if not perk:
                            def fn(e, b_=b, ps_=ps):
                                ins = None
                                for p, w in ((0, whi[b_]), (1, wlo[b_])):
                                    for k in range(NDT):
                                        ins = e.matmul(ps_[:, 0:TC], w[:, k * 128:(k + 1) * 128],
                                                       X16[:, k * TC:(k + 1) * TC],
                                                       start=(p == 0 and k == 0),
                                                       stop=(p == 1 and k == NDT - 1))
                                return ins
                            return P.emit("tensor", fn,
                                          waits=wready + [x16g_[NDT - 1], psg[psid]], inc="tP")
                        gdone = None
                        for p in range(2):
                            for k in range(NDT):
                                wts = (wready + [psg[psid]]) if (p == 0 and k == 0) else []
                                wts = wts + [x16g_[k]]
                                last = (p == 1 and k == NDT - 1)
                                gdone = P.emit("tensor",
                                               (lambda b_, ps_, p_, k_: lambda e: e.matmul(
                                                   ps_[:, 0:TC],
                                                   (whi, wlo)[p_][b_][:, k_ * 128:(k_ + 1) * 128],
                                                   X16[:, k_ * TC:(k_ + 1) * TC],
                                                   start=(p_ == 0 and k_ == 0),
                                                   stop=(p_ == 1 and k_ == NDT - 1)))(b, ps, p, k),
                                               waits=wts, inc="tP" if last else None)
                        return gdone

                    def post(gdone):
                        evd = P.emit("scalar",
                                     (lambda j_, ps_: lambda e: e.activation(
                                         out=D16[:, j_ * TC:(j_ + 1) * TC], in_=ps_[:, 0:TC],
                                         func=AF.Gelu, bias=b0_sb[:, j_:j_ + 1], scale=1.0))(j, ps),
                                     waits=[("tP", gdone), ("dG", cst_loaded)], inc="tS")
                        psg[psid] = ("tS", evd)
                        d16_g[j] = ("tS", evd)
                    return (strip(w0T, 0, j * 128, NF), group, post)

                stages = [mk_mm1(j, tc, x16_g) for j in range(NFT)]
                run_pipe(ws, stages, pend=(mm1_pend if tc == 0 else None))

                hooks = {}
                if tc + 1 < NCH:
                    xw = ("tP", P.counts["tP"])
                    def later_apply(tc_=tc, xw_=xw):
                        nonlocal x16_g
                        x16_g = emit_apply(tc_ + 1, xw_)
                    hooks.setdefault(3, []).append(later_apply)


                def mk_mm2(dt, s, tc_):
                    psid = 2 + dt % 2
                    ps = (P2, P3)[dt % 2]

                    def group(b, wready):
                        def fn(e, b_=b, ps_=ps, s_=s):
                            ins = None
                            for p, w in ((0, whi[b_]), (1, wlo[b_])):
                                for k in range(NDT):
                                    kk = s_ * 16 + k
                                    ins = e.matmul(ps_[:, 0:TC], w[:, k * 128:(k + 1) * 128],
                                                   D16[:, kk * TC:(kk + 1) * TC],
                                                   start=(s_ == 0 and p == 0 and k == 0),
                                                   stop=(s_ == 3 and p == 1 and k == NDT - 1))
                            return ins
                        return P.emit("tensor", fn,
                                      waits=wready + [d16_g[s * 16 + 15], psg[psid]], inc="tP")

                    def post(gdone):
                        eb = dt % 2
                        ev = P.emit("scalar",
                                    (lambda ps_, dt_, eb_: lambda e: e.activation(
                                        out=E16[eb_][:, :], in_=ps_[:, 0:TC], func=AF.Identity,
                                        bias=b1_sb[:, dt_:dt_ + 1], scale=1.0))(ps, dt, eb),
                                    waits=[("tP", gdone), e_guard[eb]], inc="tS")
                        radd = P.emit("vector",
                                      (lambda dt_, tc2, eb_: lambda e: e.scalar_tensor_tensor(
                                          out=xst(dt_, tc2 * TC, TC), in0=E16[eb_][:, :], scalar=0.5,
                                          in1=xst(dt_, tc2 * TC, TC), op0=ALU.mult, op1=ALU.add))(dt, tc_, eb),
                                      waits=[("tS", ev)], inc="tV")
                        e_guard[eb] = ("tV", radd)
                        psg[psid] = ("tV", radd)
                        state["xg_h"][tc_][dt] = [("tV", radd)]
                        if tc_ == NCH - 1:
                            state["xg"][dt] = [("tV", radd)]
                    return (strip(w1T, s * 2048, dt * 128, D), group,
                            post if s == 3 else None)

                stages = [mk_mm2(dt, s, tc) for dt in range(NDT) for s in range(4)]
                run_pipe(ws, stages, hooks)
            ph.close()

        def emit_attn():
            ph = ExitStack()
            kvt = sb(ph, "at_kvt", [128, NDT * NT], F32)
            q_sb = [sb(ph, f"at_q{b}", [128, TC], F32) for b in range(2)]
            wtmp = sb(ph, "at_wtmp", [128, 2048], F32)
            whi = [sb(ph, f"at_whi{b}", [128, 2048], F16) for b in range(2)]
            wlo = [sb(ph, f"at_wlo{b}", [128, 2048], F16) for b in range(2)]

            ws = WSplit(wtmp, whi, wlo)
            psg = {k: P.now("tV") for k in range(4)}

            ph_a = ExitStack()
            A16h = sb(ph_a, "at_A16h", [128, NDT * TC], F16)
            A16l = sb(ph_a, "at_A16l", [128, NDT * TC], F16)

            st_qpage = {"w": 0}
            vst = {"qc": P.counts["tV"], "pd": P.counts["tV"], "pd2": P.counts["tV"], "ah": 0}
            ln = LNSteps()

            def emit_a_apply(tc, xwar, scr):
                guards = []
                lh = ll = 0
                for i in range(NDT):
                    a1 = P.emit("vector",
                                (lambda i_, tc_, s0: lambda e: e.tensor_tensor(
                                    out=s0, in0=xst(i_, tc_ * TC, TC),
                                    in1=bcA[:, tc_ * TC:(tc_ + 1) * TC], op=ALU.mult))(i, tc, scr[0]),
                                waits=ln.fin[tc] + list(state["xg_h"][tc][i]) + [xwar, P.now("tV"), P.now("tS")],
                                inc="tV")
                    a2 = P.emit("vector",
                                (lambda i_, tc_, s0, s1: lambda e: e.tensor_tensor(
                                    out=s1, in0=s0,
                                    in1=bcB[:, tc_ * TC:(tc_ + 1) * TC], op=ALU.add))(i, tc, scr[0], scr[1]),
                                waits=[("tV", a1), ("tS", vst["ah"])], inc="tV")
                    lh = P.emit("scalar",
                                (lambda i_, s1: lambda e: e.activation(
                                    out=A16h[:, i_ * TC:(i_ + 1) * TC],
                                    in_=s1,
                                    func=AF.Copy, bias=0.0, scale=1.0))(i, scr[1]),
                                waits=[("tV", a2), P.now("tP")], inc="tS")
                    vst["ah"] = lh
                    ll = P.emit("vector",
                                (lambda i_, s1: lambda e: e.tensor_tensor(
                                    out=A16l[:, i_ * TC:(i_ + 1) * TC],
                                    in0=s1,
                                    in1=A16h[:, i_ * TC:(i_ + 1) * TC],
                                    op=ALU.subtract))(i, scr[1]),
                                waits=[("tS", lh)], inc="tV")
                    guards.append([("tS", lh), ("tV", ll)])
                state["bc_read"][tc] = ll
                return guards

            agc = 0
            rc = 0
            carry_c = 0

            def emit_carry():
                nonlocal carry_c
                agr = 0
                for hh in range(2):
                    agr = P.emit("gpsimd",
                                 (lambda hh_: lambda e: e.dma_start(
                                     out=sq_scr[0:n_cores, :],
                                     in_=ag_out.ap()[0:n_cores, hh_ * 1024:(hh_ + 1) * 1024]))(hh),
                                 waits=[("tG", agc), ("tP", carry_c), ("tS", P.counts["tS"]),
                                        ("tV", P.counts["tV"])],
                                 inc="dA", inc_amt=16)
                    for ii in range(8):
                        i = hh * 8 + ii
                        carry_c = P.emit("tensor",
                                         (lambda i_, ii_: lambda e: e.matmul(
                                             P3[0:128, i_:i_ + 1],
                                             sq_scr[0:n_cores, ii_ * 128:(ii_ + 1) * 128],
                                             csel_sb[0:n_cores, 0:1], start=True, stop=True))(i, ii),
                                         waits=[("dA", agr), ("dG", cst_loaded), ("tV", rc),
                                                psg[3]],
                                         inc="tP")
                cc = P.emit("vector", lambda e: e.tensor_copy(out=carry_sb[:, :], in_=P3[0:128, 0:NDT]),
                            waits=[("tP", carry_c)], inc="tV")
                psg[3] = ("tV", cc)
                return cc

            scan_g = {}
            cc_g = {"cc": 0}

            def emit_scan(i):
                sc = P.emit("vector",
                            (lambda i_: lambda e: e.tensor_tensor_scan(
                                out=kvt[:, i_ * NT: i_ * NT + NT],
                                data0=ones_sc[:, :], data1=kvt[:, i_ * NT: i_ * NT + NT],
                                initial=carry_sb[:, i_:i_ + 1],
                                op0=ALU.mult, op1=ALU.add))(i),
                            waits=[("tV", cc_g["cc"]), ("tV", m1)], inc="tV")
                scan_g[i] = sc
            for i in range(NDT):
                ln.step(i, 0)
            ln.finish_half(0)
            kv_pend = {0: ws.load(strip(wkT, 0, 0, D), 0),
                       1: ws.load(strip(wvT, 0, 0, D), 1)}
            a_guards0 = emit_a_apply(0, ("tP", P.counts["tP"]),
                                     (kvt[:, 0:TC], kvt[:, TC:2 * TC]))
            for i in range(NDT):
                ln.step(i, 1)
            ln.finish_half(1)
            for tc in range(NCH):
                if tc > 0:
                    kv_pend = {0: ws.load(strip(wkT, 0, 0, D), 0),
                               1: ws.load(strip(wvT, 0, 0, D), 1)}
                a_guards = (a_guards0 if tc == 0 else
                            emit_a_apply(tc, ("tP", P.counts["tP"]),
                                         (sq_scr[:, 0:TC], sq_scr[:, TC:2 * TC])))

                def mk_proj(i, pj, tc_, a_g, seq, perk):
                    wT = (wqT, wkT, wvT)[pj]
                    if pj == 1 or pj == 0:
                        psid = seq % 2
                        ps = (P0, P1)[psid]
                    else:
                        psid = 2 + seq % 2
                        ps = (P2, P3)[psid - 2]

                    def group(b, wready):
                        if not perk:
                            def fn(e, b_=b, ps_=ps):
                                ins = None
                                terms = ((whi[b_], A16h), (whi[b_], A16l), (wlo[b_], A16h))
                                for p, (w, a) in enumerate(terms):
                                    for k in range(NDT):
                                        ins = e.matmul(ps_[:, 0:TC], w[:, k * 128:(k + 1) * 128],
                                                       a[:, k * TC:(k + 1) * TC],
                                                       start=(p == 0 and k == 0),
                                                       stop=(p == 2 and k == NDT - 1))
                                return ins
                            return P.emit("tensor", fn,
                                          waits=wready + a_g[NDT - 1] + [psg[psid]], inc="tP")
                        gdone = None
                        terms = ((0, 0), (0, 1), (1, 0))
                        for p, (wsel, asel) in enumerate(terms):
                            for k in range(NDT):
                                wts = (wready + [psg[psid]]) if (p == 0 and k == 0) else []
                                wts = wts + a_g[k]
                                last = (p == 2 and k == NDT - 1)
                                gdone = P.emit("tensor",
                                               (lambda b_, ps_, p_, k_, ws_, as_: lambda e: e.matmul(
                                                   ps_[:, 0:TC],
                                                   (whi, wlo)[ws_][b_][:, k_ * 128:(k_ + 1) * 128],
                                                   (A16h, A16l)[as_][:, k_ * TC:(k_ + 1) * TC],
                                                   start=(p_ == 0 and k_ == 0),
                                                   stop=(p_ == 2 and k_ == NDT - 1)))(
                                                       b, ps, p, k, wsel, asel),
                                               waits=wts,
                                               inc="tP" if last else None)
                        return gdone

                    def post_q(gdone, ps_=None, psid_=None):
                        qc = P.emit("vector",
                                    (lambda ps2: lambda e: e.tensor_copy(
                                        out=q_sb[0][:, :], in_=ps2[:, 0:TC]))(ps_),
                                    waits=[("tP", gdone), ("dQ", st_qpage["w"])], inc="tV")
                        psg[psid_] = ("tV", qc)
                        st_qpage["w"] = P.emit("gpsimd",
                                               (lambda i_, tc2: lambda e: e.dma_start(
                                                   out=hbm_tile_slice(q_page, i_, tc2 * TC, TC),
                                                   in_=q_sb[0][:, :]))(i, tc_),
                                               waits=[("tV", qc)], inc="dQ", inc_amt=16)

                    def post_v(gdone, ps_=None, psid_=None):
                        scr = sq_scr[:, (i % 2) * TC:(i % 2) * TC + TC]
                        kps = (P0, P1)[i % 2]
                        vev = P.emit("vector",
                                     (lambda ps2, scr_: lambda e: e.tensor_copy(
                                         out=scr_, in_=ps2[:, 0:TC]))(ps_, scr),
                                     waits=[("tP", gdone), ("tV", vst["pd2"])], inc="tV")
                        pdone = P.emit("vector",
                                       (lambda i_, tc2, kps_, scr_: lambda e: e.scalar_tensor_tensor(
                                           out=kvt[:, i_ * NT + tc2 * TC: i_ * NT + (tc2 + 1) * TC],
                                           in0=kps_[:, 0:TC], scalar=1.0, in1=scr_,
                                           op0=ALU.mult, op1=ALU.mult,
                                           accum_out=s1acc[:, tc2 * NDT + i_: tc2 * NDT + i_ + 1]))(
                                               i, tc_, kps, scr),
                                       waits=[("tV", vev)], inc="tV")
                        vst["pd2"] = vst["pd"]
                        vst["pd"] = pdone
                        psg[i % 2] = ("tV", pdone)
                        psg[psid_] = ("tV", vev)
                    post0 = (post_q, None, post_v)[pj]
                    post = (None if post0 is None
                            else (lambda g, p2=ps, pi=psid, f=post0: f(g, p2, pi)))
                    return (strip(wT, 0, i * 128, D), group, post)

                kv_stages = [mk_proj(i, pj, tc, a_guards, i, i < 1) for i in range(NDT) for pj in (1, 2)]
                q_stages = [mk_proj(i, 0, tc, a_guards, i, False) for i in range(NDT)]
                run_pipe(ws, kv_stages, pend=kv_pend)
                if tc == NCH - 1:
                    p_all = ("tV", vst["pd"])
                    s1d = P.emit("vector", lambda e: e.tensor_tensor(
                        out=s1T[:, :], in0=s1acc[:, 0:NDT], in1=s1acc[:, NDT:2 * NDT], op=ALU.add),
                        waits=[p_all], inc="tV")
                    tr = P.emit("tensor", lambda e: e.transpose(
                        P3[0:16, 0:128], s1T[:, :], ident_sb[:, :]),
                        waits=[("tV", s1d), psg[3], ("dG", cst_loaded)], inc="tP")
                    rc = P.emit("vector", lambda e: e.tensor_copy(out=s1row[:, :], in_=P3[0:16, 0:128]),
                                waits=[("tP", tr)], inc="tV")
                    agw = P.emit("gpsimd", lambda e: e.dma_start(
                        out=ag_in.ap().rearrange("(i p) -> i p", p=128), in_=s1row[:, :]),
                        waits=[("tV", rc)], inc="dG", inc_amt=16)
                    agc = P.emit("gpsimd", lambda e: e.collective_compute(
                        "AllGather", ALU.bypass,
                        replica_groups=[list(range(n_cores))],
                        ins=[ag_in.ap()],
                        outs=[ag_out.ap()]),
                        waits=[("dG", agw)], inc="tG")
                qhooks = {}
                if tc == NCH - 1:
                    def hk_carry():
                        cc_g["cc"] = emit_carry()
                    qhooks.setdefault(3, []).append(hk_carry)
                    for jj in range(8):
                        qhooks.setdefault(4 + jj, []).append(
                            (lambda j2: lambda: (emit_scan(2 * j2), emit_scan(2 * j2 + 1)))(jj))
                run_pipe(ws, q_stages, qhooks)

            ph_a.close()
            vh_h = sb(ph, "at_vhh", [128, NDT * TC], F16)
            vh_l = sb(ph, "at_vhl", [128, NDT * TC], F16)

            qsem = ("dR", "dR1")
            prev_vm = [P.counts["tV"], P.counts["tV"]]
            qr = {}
            for tc in range(NCH):
                wo_pend = {0: ws.load(strip(woT, 0, 0, D), 0),
                           1: ws.load(strip(woT, 0, 128, D), 1)}
                vh_guard = []

                def emit_qr(i_abs):
                    tcq, iq = divmod(i_abs, NDT)
                    qb = i_abs % 2
                    qr[i_abs] = P.emit("gpsimd",
                                       (lambda iq_, tcq_, qb_: lambda e: e.dma_start(
                                           out=q_sb[qb_][:, :],
                                           in_=hbm_tile_slice(q_page, iq_, tcq_ * TC, TC)))(iq, tcq, qb),
                                       waits=[("dQ", st_qpage["w"]), ("tV", prev_vm[qb])],
                                       inc=qsem[qb], inc_amt=16)
                if tc == 0:
                    emit_qr(0)
                    emit_qr(1)
                for i in range(NDT):
                    i_abs = tc * NDT + i
                    qb = i_abs % 2
                    vm = P.emit("vector",
                                (lambda i_, tc_, qb_: lambda e: e.tensor_tensor(
                                    out=kvt[:, i_ * NT + tc_ * TC: i_ * NT + (tc_ + 1) * TC],
                                    in0=q_sb[qb_][:, :],
                                    in1=kvt[:, i_ * NT + tc_ * TC: i_ * NT + (tc_ + 1) * TC],
                                    op=ALU.mult))(i, tc, qb),
                                waits=[(qsem[qb], qr[i_abs]),
                                       ("tV", scan_g[i])], inc="tV")
                    prev_vm[qb] = vm
                    if i_abs + 2 < 2 * NDT:
                        emit_qr(i_abs + 2)
                    vh1 = P.emit("scalar",
                                 (lambda i_, tc_: lambda e: e.activation(
                                     out=vh_h[:, i_ * TC:(i_ + 1) * TC],
                                     in_=kvt[:, i_ * NT + tc_ * TC: i_ * NT + (tc_ + 1) * TC],
                                     func=AF.Copy, bias=0.0, scale=1.0))(i, tc),
                                 waits=[("tV", vm), P.now("tP")], inc="tS")
                    vh2 = P.emit("vector",
                                 (lambda i_, tc_: lambda e: e.tensor_tensor(
                                     out=vh_l[:, i_ * TC:(i_ + 1) * TC],
                                     in0=kvt[:, i_ * NT + tc_ * TC: i_ * NT + (tc_ + 1) * TC],
                                     in1=vh_h[:, i_ * TC:(i_ + 1) * TC],
                                     op=ALU.subtract))(i, tc),
                                 waits=[("tS", vh1)], inc="tV")
                    vh_guard.append([("tS", vh1), ("tV", vh2)])

                def mk_oproj(f, tc_):
                    psid = f % 2
                    ps = (P0, P1)[psid]

                    def group(b, wready):
                        gdone = None
                        terms = ((0, 0), (0, 1), (1, 0))
                        for p, (wsel, asel) in enumerate(terms):
                            for k in range(NDT):
                                wts = []
                                if p == 0 and k == 0:
                                    wts = wready + [psg[psid]]
                                wts = wts + vh_guard[k]
                                last = (p == 2 and k == NDT - 1)
                                gdone = P.emit("tensor",
                                               (lambda b_, ps_, p_, k_, ws_, as_: lambda e: e.matmul(
                                                   ps_[:, 0:TC],
                                                   (whi, wlo)[ws_][b_][:, k_ * 128:(k_ + 1) * 128],
                                                   (vh_h, vh_l)[as_][:, k_ * TC:(k_ + 1) * TC],
                                                   start=(p_ == 0 and k_ == 0),
                                                   stop=(p_ == 2 and k_ == NDT - 1)))(
                                                       b, ps, p, k, wsel, asel),
                                               waits=wts,
                                               inc="tP" if last else None)
                        return gdone

                    def post(gdone):
                        radd = P.emit("vector",
                                      (lambda f_, tc2, ps_: lambda e: e.scalar_tensor_tensor(
                                          out=xst(f_, tc2 * TC, TC), in0=ps_[:, 0:TC], scalar=DENOM,
                                          in1=xst(f_, tc2 * TC, TC), op0=ALU.mult, op1=ALU.add))(
                                              f, tc_, ps),
                                      waits=[("tP", gdone)], inc="tV")
                        psg[psid] = ("tV", radd)
                        state["xg_h"][tc_][f] = [("tV", radd)]
                        if tc_ == NCH - 1:
                            state["xg"][f] = [("tV", radd)]
                    return (strip(woT, 0, f * 128, D), group, post)

                stages = [mk_oproj(f, tc) for f in range(NDT)]
                run_pipe(ws, stages, pend=wo_pend)
            ph.close()

        emit_ffn("f1")
        emit_attn()
        emit_ffn("f2")

        yw = P.emit("gpsimd", lambda e: e.dma_start(out=hbm_tiles(y), in_=xs[:, :]),
                    waits=[g for gs in state["xg"] for g in gs], inc="dG", inc_amt=16)
        P.emit("gpsimd", lambda e: e.memset(s1row[0:1, 0:1], 0.0), waits=[("dG", yw)])

        with nc.Block() as block:
            @block.sync
            def _(eng):
                P.run("sync", eng)

            @block.scalar
            def _(eng):
                P.run("scalar", eng)

            @block.vector
            def _(eng):
                P.run("vector", eng)

            @block.tensor
            def _(eng):
                P.run("tensor", eng)

            @block.gpsimd
            def _(eng):
                P.run("gpsimd", eng)

    return nc


def make_in_maps(L0_attn_x, L0_attn_Wq, L0_attn_Wk, L0_attn_Wv, L0_attn_Wo,
                 L0_ffn_W0, L0_ffn_bias0, L0_ffn_W1, L0_ffn_bias1, n_cores=8):
    x = np.asarray(L0_attn_x, dtype=np.float32)
    wqT = np.ascontiguousarray(np.asarray(L0_attn_Wq, np.float32).T)
    wkT = np.ascontiguousarray(np.asarray(L0_attn_Wk, np.float32).T)
    wvT = np.ascontiguousarray(np.asarray(L0_attn_Wv, np.float32).T)
    woT = np.ascontiguousarray(np.asarray(L0_attn_Wo, np.float32).T)
    w0T = np.ascontiguousarray(np.asarray(L0_ffn_W0, np.float32).T)
    w1T = np.ascontiguousarray(np.asarray(L0_ffn_W1, np.float32).T)
    b0c = np.ascontiguousarray(np.asarray(L0_ffn_bias0, np.float32).reshape(NFT, 128).T)
    b1c = np.ascontiguousarray(np.asarray(L0_ffn_bias1, np.float32).reshape(NDT, 128).T)
    ident = np.eye(128, dtype=np.float32)
    in_maps = []
    for c in range(n_cores):
        b, h = c // 2, c % 2
        xTc = np.ascontiguousarray(x[b, h * NT:(h + 1) * NT, :].T)
        cs = np.zeros((8, 1), np.float32)
        if c % 2 == 1:
            cs[c - 1, 0] = 1.0
        in_maps.append({
            "xT": xTc, "wqT": wqT, "wkT": wkT, "wvT": wvT, "woT": woT,
            "w0T": w0T, "w1T": w1T, "b0c": b0c, "b1c": b1c,
            "csel": cs, "ident": ident,
        })
    return in_maps


def kernel(L0_attn_x, L0_attn_Wq, L0_attn_Wk, L0_attn_Wv, L0_attn_Wo,
           L0_ffn_W0, L0_ffn_bias0, L0_ffn_W1, L0_ffn_bias1):
    n_cores = 8
    nc = build_program(n_cores)
    in_maps = make_in_maps(L0_attn_x, L0_attn_Wq, L0_attn_Wk, L0_attn_Wv, L0_attn_Wo,
                           L0_ffn_W0, L0_ffn_bias0, L0_ffn_W1, L0_ffn_bias1, n_cores)
    res = run_bass_kernel_spmd(nc, in_maps, core_ids=list(range(n_cores)))
    out = np.empty((4, 2048, 2048), np.float32)
    for c in range(n_cores):
        b, h = c // 2, c % 2
        out[b, h * NT:(h + 1) * NT, :] = np.asarray(res.results[c]["y"]).T
    return out


# version 13
# speedup vs baseline: 1.0237x; 1.0237x over previous
import numpy as np
import concourse.bass as bass
import concourse.mybir as mybir
from concourse.bass_utils import run_bass_kernel_spmd
from contextlib import ExitStack

F32 = mybir.dt.float32
F16 = mybir.dt.float16
AF = mybir.ActivationFunctionType
ALU = mybir.AluOpType

D = 2048
NDT = 16
NT = 1024
TC = 512
NCH = NT // TC
NF = 8192
NFT = 64
DENOM = float(np.float32(np.sqrt(np.float32(2048.0))) / np.float32(2048.0))
EPS = 1e-5
RECIP_D = float(np.float32(1.0) / np.float32(2048.0))

ENGINES = ("sync", "scalar", "vector", "tensor", "gpsimd")


class Plan:

    def __init__(self, nc, sems):
        self.nc = nc
        self.sems = sems
        self.counts = {k: 0 for k in sems}
        self.items = {e: [] for e in ENGINES}
        self.last_wait = {}

    def emit(self, eng, fn, waits=(), inc=None, inc_amt=1):
        wl = []
        for w in waits:
            if w is None:
                continue
            s, v = w
            if v <= 0:
                continue
            assert v <= self.counts[s], (s, v, self.counts[s])
            key = (eng, s)
            if self.last_wait.get(key, -1) >= v:
                continue
            self.last_wait[key] = v
            wl.append((s, v))
        after = None
        if inc is not None:
            self.counts[inc] += inc_amt
            after = self.counts[inc]
        self.items[eng].append((wl, fn, inc, inc_amt))
        return after

    def now(self, sem):
        return (sem, self.counts[sem])

    def run(self, eng_name, eng):
        for wl, fn, inc, inc_amt in self.items[eng_name]:
            for s, v in wl:
                eng.wait_ge(self.sems[s], v)
            ins = fn(eng)
            if inc is not None:
                ins.then_inc(self.sems[inc], inc_amt)


def build_program(n_cores=8):
    nc = bass.Bass("TRN2", target_bir_lowering=False)

    xT = nc.dram_tensor("xT", [D, NT], F32, kind="ExternalInput")
    wqT = nc.dram_tensor("wqT", [D, D], F32, kind="ExternalInput")
    wkT = nc.dram_tensor("wkT", [D, D], F32, kind="ExternalInput")
    wvT = nc.dram_tensor("wvT", [D, D], F32, kind="ExternalInput")
    woT = nc.dram_tensor("woT", [D, D], F32, kind="ExternalInput")
    w0T = nc.dram_tensor("w0T", [D, NF], F32, kind="ExternalInput")
    w1T = nc.dram_tensor("w1T", [NF, D], F32, kind="ExternalInput")
    b0c = nc.dram_tensor("b0c", [128, NFT], F32, kind="ExternalInput")
    b1c = nc.dram_tensor("b1c", [128, NDT], F32, kind="ExternalInput")
    csel = nc.dram_tensor("csel", [8, 1], F32, kind="ExternalInput")
    ident = nc.dram_tensor("ident", [128, 128], F32, kind="ExternalInput")
    y = nc.dram_tensor("y", [D, NT], F32, kind="ExternalOutput")
    q_page = nc.dram_tensor("q_page", [D, NT], F32, kind="Internal")
    ag_in = nc.dram_tensor("ag_in", [D], F32, kind="Internal")
    ag_out = nc.dram_tensor("ag_out", [n_cores, D], F32, kind="Internal")

    def hbm_tiles(t, ncols=NT):
        return bass.AP(t, 0, [[ncols, 128], [128 * ncols, NDT], [1, ncols]])

    def hbm_tile_slice(t, i, c0, w, ncols=NT):
        return bass.AP(t, i * 128 * ncols + c0, [[ncols, 128], [1, w]])

    def strip(t, row0, col0, tot_cols, ncols=128):
        return bass.AP(t, row0 * tot_cols + col0,
                       [[tot_cols, 128], [128 * tot_cols, NDT], [1, ncols]])

    with ExitStack() as st:
        def sb(stk, name, shape, dt):
            return stk.enter_context(nc.sbuf_tensor(name, shape, dt))

        sems = {}
        for s in ("dS0", "dS1", "dG", "dQ", "dR", "dR1", "dA", "tS", "tV", "tP", "tG"):
            sems[s] = st.enter_context(nc.semaphore(s))
        P = Plan(nc, sems)

        xs = sb(st, "xs", [128, NDT * NT], F32)
        bcA = sb(st, "bcA", [128, NT], F32)
        bcB = sb(st, "bcB", [128, NT], F32)
        b0_sb = sb(st, "b0_sb", [128, NFT], F32)
        b1_sb = sb(st, "b1_sb", [128, NDT], F32)
        csel_sb = sb(st, "csel_sb", [8, 1], F32)
        ident_sb = sb(st, "ident_sb", [128, 128], F32)
        ones_sc = sb(st, "ones_sc", [128, NT], F16)
        onescol = sb(st, "onescol", [1, 128], F32)
        onesrow = sb(st, "onesrow", [128, 1], F32)
        statrow = sb(st, "statrow", [1, 512], F32)
        s1acc = sb(st, "s1acc", [128, NDT * NCH], F32)
        s1T = sb(st, "s1T", [128, NDT], F32)
        s1row = sb(st, "s1row", [NDT, 128], F32)
        carry_sb = sb(st, "carry_sb", [128, NDT], F32)
        sq_scr = sb(st, "sq_scr", [128, NT], F32)

        P0 = st.enter_context(nc.psum_tensor("P0", [128, 512], F32))
        P1 = st.enter_context(nc.psum_tensor("P1", [128, 512], F32))
        P2 = st.enter_context(nc.psum_tensor("P2", [128, 512], F32))
        P3 = st.enter_context(nc.psum_tensor("P3", [128, 512], F32))
        PB = st.enter_context(nc.psum_tensor("PB", [128, 1024], F32))
        PR1 = st.enter_context(nc.psum_tensor("PR1", [1, 512], F32))
        PR2 = st.enter_context(nc.psum_tensor("PR2", [1, 512], F32))

        def xst(i, t0=0, w=NT):
            return xs[:, i * NT + t0: i * NT + t0 + w]

        x_loaded = P.emit("gpsimd", lambda e: e.dma_start(out=xs[:, :], in_=hbm_tiles(xT)),
                          inc="dG", inc_amt=16)
        c1 = P.emit("gpsimd", lambda e: e.dma_start(out=b0_sb[:, :], in_=b0c.ap()),
                    waits=[("dG", x_loaded)], inc="dG", inc_amt=16)
        c2 = P.emit("gpsimd", lambda e: e.dma_start(out=b1_sb[:, :], in_=b1c.ap()),
                    waits=[("dG", c1)], inc="dG", inc_amt=16)
        c3 = P.emit("gpsimd", lambda e: e.dma_start(out=csel_sb[:, :], in_=csel.ap()),
                    waits=[("dG", c2)], inc="dG", inc_amt=16)
        cst_loaded = P.emit("gpsimd", lambda e: e.dma_start(out=ident_sb[:, :], in_=ident.ap()),
                            waits=[("dG", c3)], inc="dG", inc_amt=16)
        m1 = P.emit("vector", lambda e: e.memset(ones_sc[:, :], 1.0), inc="tV")
        m2 = P.emit("vector", lambda e: e.memset(onescol[:, :], 1.0), inc="tV")
        m3 = P.emit("vector", lambda e: e.memset(onesrow[:, :], 1.0), inc="tV")

        state = {"xg": [[("dG", x_loaded)] for _ in range(NDT)],
                 "xg_h": [[[("dG", x_loaded)] for _ in range(NDT)] for _ in range(2)],
                 "bc_read": [0, 0], "ln_prev": {}}

        class LNSteps:

            def __init__(self, slots=None):
                self.slots = slots or [sq_scr[:, 0:512], sq_scr[:, 512:1024]]
                self.ns = len(self.slots)
                self.floor = [("tP", P.counts["tP"]), ("tV", P.counts["tV"]),
                              ("tS", P.counts["tS"])]
                self.sum_last = 0
                self.sq_last = 0
                self.sq_mm = [0] * self.ns
                self.sq_op = [0] * self.ns
                self.scr_guard = 0
                self.fin = {}
                prev = state.get("ln_prev", {})
                self.bmm = prev.get("bmm", 0)
                self.pr_guard = prev.get("pr_guard", [])

            def step(self, i, half):
                xgw = list(state["xg_h"][half][i])
                par = (half * NDT + i) % self.ns
                slot = self.slots[par]
                sq = P.emit("scalar",
                            (lambda i_, h, sl: lambda e: e.activation(
                                out=sl,
                                in_=xs[:, i_ * NT + h * 512: i_ * NT + h * 512 + 512],
                                func=AF.Square, bias=0.0, scale=1.0))(i, half, slot),
                            waits=xgw + self.floor + [("tP", self.sq_mm[par]),
                                                      ("tS", self.sq_op[par]),
                                                      ("tV", self.scr_guard)],
                            inc="tS")
                self.sq_op[par] = sq
                first_w = self.pr_guard if i == 0 else []
                self.sum_last = P.emit(
                    "tensor",
                    (lambda i_, h: lambda e: e.matmul(
                        PR1[0:1, 0:512], onesrow[:, :],
                        xs[:, i_ * NT + h * 512: i_ * NT + h * 512 + 512],
                        start=(i_ == 0), stop=(i_ == NDT - 1)))(i, half),
                    waits=xgw + [("tV", m3)] + first_w,
                    inc="tP" if i == NDT - 1 else None)
                self.sq_mm[par] = P.emit(
                    "tensor",
                    (lambda i_, sl: lambda e: e.matmul(
                        PR2[0:1, 0:512], onesrow[:, :], sl,
                        start=(i_ == 0), stop=(i_ == NDT - 1)))(i, slot),
                    waits=[("tS", sq)] + first_w, inc="tP")
                self.sq_last = self.sq_mm[par]

            def finish_half(self, h):
                hs = slice(h * 512, (h + 1) * 512)
                bcr = state["bc_read"][h]
                cp1 = P.emit("vector",
                             (lambda: lambda e: e.tensor_copy(
                                 out=statrow[0:1, 0:512], in_=PR1[0:1, 0:512]))(),
                             waits=[("tP", self.sum_last), ("tP", self.bmm)], inc="tV")
                bmm_a = P.emit("tensor",
                               (lambda h_: lambda e: e.matmul(
                                   PB[:, h_ * 512:(h_ + 1) * 512], onescol[:, :],
                                   statrow[0:1, 0:512], start=True, stop=True))(h),
                               waits=[("tV", cp1), ("tV", m2)], inc="tP")
                meanv = P.emit("vector",
                               (lambda h_: lambda e: e.tensor_scalar(
                                   out=bcB[:, h_ * 512:(h_ + 1) * 512],
                                   in0=PB[:, h_ * 512:(h_ + 1) * 512],
                                   scalar1=RECIP_D, scalar2=None, op0=ALU.mult))(h),
                               waits=[("tP", bmm_a), ("tV", bcr)], inc="tV")
                cp2 = P.emit("vector",
                             (lambda: lambda e: e.tensor_copy(
                                 out=statrow[0:1, 0:512], in_=PR2[0:1, 0:512]))(),
                             waits=[("tP", self.sq_last), ("tP", bmm_a)], inc="tV")
                bmm_b = P.emit("tensor",
                               (lambda h_: lambda e: e.matmul(
                                   PB[:, h_ * 512:(h_ + 1) * 512], onescol[:, :],
                                   statrow[0:1, 0:512], start=True, stop=True))(h),
                               waits=[("tV", cp2), ("tV", meanv)], inc="tP")
                t1 = P.emit("vector",
                            (lambda h_: lambda e: e.tensor_scalar(
                                out=sq_scr[:, 0:512], in0=PB[:, h_ * 512:(h_ + 1) * 512],
                                scalar1=RECIP_D, scalar2=None, op0=ALU.mult))(h),
                            waits=[("tP", bmm_b), ("tP", self.sq_last)], inc="tV")
                t2 = P.emit("vector",
                            (lambda h_: lambda e: e.tensor_tensor(
                                out=bcA[:, h_ * 512:(h_ + 1) * 512],
                                in0=bcB[:, h_ * 512:(h_ + 1) * 512],
                                in1=bcB[:, h_ * 512:(h_ + 1) * 512], op=ALU.mult))(h),
                            waits=[("tV", meanv), ("tV", bcr)], inc="tV")
                t3 = P.emit("vector",
                            (lambda h_: lambda e: e.tensor_tensor(
                                out=bcA[:, h_ * 512:(h_ + 1) * 512], in0=sq_scr[:, 0:512],
                                in1=bcA[:, h_ * 512:(h_ + 1) * 512], op=ALU.subtract))(h),
                            waits=[("tV", t1), ("tV", t2)], inc="tV")
                t3e = P.emit("vector",
                             (lambda h_: lambda e: e.tensor_scalar(
                                 out=bcA[:, h_ * 512:(h_ + 1) * 512],
                                 in0=bcA[:, h_ * 512:(h_ + 1) * 512],
                                 scalar1=EPS, scalar2=None, op0=ALU.add))(h),
                             waits=[("tV", t3)], inc="tV")
                t4 = P.emit("scalar",
                            (lambda h_: lambda e: e.activation(
                                out=bcA[:, h_ * 512:(h_ + 1) * 512],
                                in_=bcA[:, h_ * 512:(h_ + 1) * 512],
                                func=AF.Sqrt, bias=0.0, scale=1.0))(h),
                            waits=[("tV", t3e)], inc="tS")
                t4b = P.emit("vector",
                             (lambda h_: lambda e: e.reciprocal(
                                 out=bcA[:, h_ * 512:(h_ + 1) * 512],
                                 in_=bcA[:, h_ * 512:(h_ + 1) * 512]))(h),
                             waits=[("tS", t4)], inc="tV")
                t5 = P.emit("vector",
                            (lambda h_: lambda e: e.scalar_tensor_tensor(
                                out=bcB[:, h_ * 512:(h_ + 1) * 512],
                                in0=bcB[:, h_ * 512:(h_ + 1) * 512], scalar=-1.0,
                                in1=bcA[:, h_ * 512:(h_ + 1) * 512],
                                op0=ALU.mult, op1=ALU.mult))(h),
                            waits=[("tV", t4b)], inc="tV")
                self.scr_guard = t3
                self.bmm = bmm_b
                self.fin[h] = [("tV", t4b), ("tV", t5)]
                if h == 1:
                    state["ln_prev"] = {"bmm": bmm_b,
                                        "pr_guard": [("tV", cp1), ("tV", cp2)]}

        class WSplit:

            def __init__(self, wtmp, whi, wlo):
                self.wtmp = wtmp if isinstance(wtmp, list) else [wtmp]
                self.nw = len(self.wtmp)
                self.whi, self.wlo = whi, wlo
                self.hi = [P.counts["tS"]] * 2
                self.lo = [P.counts["tV"]] * 2
                self.pe = [P.counts["tP"]] * 2
                self.wh = [P.counts["tS"]] * self.nw
                self.wl = [P.counts["tV"]] * self.nw
                self.seq = 0

            def load(self, src_ap, b):
                w = self.seq % self.nw
                self.seq += 1
                sem = ("dS0", "dS1")[w]
                d = P.emit("sync",
                           (lambda ap_, w_: lambda e: e.dma_start(
                               out=self.wtmp[w_][:, :], in_=ap_))(src_ap, w),
                           waits=[("tS", self.wh[w]), ("tV", self.wl[w])],
                           inc=sem, inc_amt=16)
                self.hi[b] = P.emit("scalar",
                                    (lambda b_, w_: lambda e: e.activation(
                                        out=self.whi[b_][:, :], in_=self.wtmp[w_][:, :],
                                        func=AF.Copy, bias=0.0, scale=1.0))(b, w),
                                    waits=[(sem, d), ("tP", self.pe[b])], inc="tS")
                self.lo[b] = P.emit("vector",
                                    (lambda b_, w_: lambda e: e.tensor_tensor(
                                        out=self.wlo[b_][:, :], in0=self.wtmp[w_][:, :],
                                        in1=self.whi[b_][:, :], op=ALU.subtract))(b, w),
                                    waits=[("tS", self.hi[b]), ("tP", self.pe[b])], inc="tV")
                self.wh[w] = self.hi[b]
                self.wl[w] = self.lo[b]
                return [("tS", self.hi[b]), ("tV", self.lo[b])]

        def run_pipe(ws, stages, hooks=None, pend=None):
            pend = dict(pend) if pend else {}
            n = len(stages)
            for idx in range(min(2, n)):
                if idx not in pend:
                    pend[idx] = ws.load(stages[idx][0], idx % 2)
            for idx in range(n):
                b = idx % 2
                _, group, post = stages[idx]
                gdone = group(b, pend[idx])
                ws.pe[b] = gdone
                if idx + 2 < n:
                    pend[idx + 2] = ws.load(stages[idx + 2][0], (idx + 2) % 2)
                if post is not None:
                    post(gdone)
                if hooks and idx in hooks:
                    for fn in hooks[idx]:
                        fn()

        def emit_ffn(tag):
            ph = ExitStack()
            X16 = sb(ph, tag + "_X16", [128, NDT * TC], F16)
            D16 = sb(ph, tag + "_D16", [128, NFT * TC], F16)
            wtmp = [sb(ph, tag + f"_wtmp{w}", [128, 2048], F32) for w in range(2)]
            whi = [sb(ph, tag + f"_whi{b}", [128, 2048], F16) for b in range(2)]
            wlo = [sb(ph, tag + f"_wlo{b}", [128, 2048], F16) for b in range(2)]
            E16 = [sb(ph, tag + f"_E16{b}", [128, TC], F16) for b in range(2)]
            ap_scr = sb(ph, tag + "_apscr", [128, TC], F32)

            ws = WSplit(wtmp, whi, wlo)
            psg = {k: P.now("tV") for k in range(4)}
            e_guard = [P.now("tV"), P.now("tV")]
            ln = LNSteps(slots=[D16[:, k * 1024:(k + 1) * 1024].bitcast(F32)
                                for k in range(4)])

            def emit_apply(tc, xwar):
                guards = []
                done = 0
                for i in range(NDT):
                    a1 = P.emit("vector",
                                (lambda i_, tc_: lambda e: e.tensor_tensor(
                                    out=ap_scr[:, 0:TC], in0=xst(i_, tc_ * TC, TC),
                                    in1=bcA[:, tc_ * TC:(tc_ + 1) * TC], op=ALU.mult))(i, tc),
                                waits=ln.fin[tc] + list(state["xg_h"][tc][i]) + [xwar, P.now("tV")],
                                inc="tV")
                    done = P.emit("vector",
                                  (lambda i_, tc_: lambda e: e.tensor_tensor(
                                      out=X16[:, i_ * TC:(i_ + 1) * TC], in0=ap_scr[:, 0:TC],
                                      in1=bcB[:, tc_ * TC:(tc_ + 1) * TC], op=ALU.add))(i, tc),
                                  waits=[("tV", a1)], inc="tV")
                    guards.append(("tV", done))
                state["bc_read"][tc] = done
                return guards

            for i in range(NDT):
                ln.step(i, 0)
            ln.finish_half(0)
            mm1_pend = {0: ws.load(strip(w0T, 0, 0, NF), 0),
                        1: ws.load(strip(w0T, 0, 128, NF), 1)}
            x16_g = emit_apply(0, ("tP", 0))
            for i in range(NDT):
                ln.step(i, 1)
            ln.finish_half(1)
            for tc in range(NCH):
                d16_g = {}

                def mk_mm1(j, tc_, x16g_):
                    psid = j % 2
                    ps = (P0, P1)[psid]
                    perk = (j == 0)

                    def group(b, wready):
                        if not perk:
                            def fn(e, b_=b, ps_=ps):
                                ins = None
                                for p, w in ((0, whi[b_]), (1, wlo[b_])):
                                    for k in range(NDT):
                                        ins = e.matmul(ps_[:, 0:TC], w[:, k * 128:(k + 1) * 128],
                                                       X16[:, k * TC:(k + 1) * TC],
                                                       start=(p == 0 and k == 0),
                                                       stop=(p == 1 and k == NDT - 1))
                                return ins
                            return P.emit("tensor", fn,
                                          waits=wready + [x16g_[NDT - 1], psg[psid]], inc="tP")
                        gdone = None
                        for p in range(2):
                            for k in range(NDT):
                                wts = (wready + [psg[psid]]) if (p == 0 and k == 0) else []
                                wts = wts + [x16g_[k]]
                                last = (p == 1 and k == NDT - 1)
                                gdone = P.emit("tensor",
                                               (lambda b_, ps_, p_, k_: lambda e: e.matmul(
                                                   ps_[:, 0:TC],
                                                   (whi, wlo)[p_][b_][:, k_ * 128:(k_ + 1) * 128],
                                                   X16[:, k_ * TC:(k_ + 1) * TC],
                                                   start=(p_ == 0 and k_ == 0),
                                                   stop=(p_ == 1 and k_ == NDT - 1)))(b, ps, p, k),
                                               waits=wts, inc="tP" if last else None)
                        return gdone

                    def post(gdone):
                        evd = P.emit("scalar",
                                     (lambda j_, ps_: lambda e: e.activation(
                                         out=D16[:, j_ * TC:(j_ + 1) * TC], in_=ps_[:, 0:TC],
                                         func=AF.Gelu, bias=b0_sb[:, j_:j_ + 1], scale=1.0))(j, ps),
                                     waits=[("tP", gdone), ("dG", cst_loaded)], inc="tS")
                        psg[psid] = ("tS", evd)
                        d16_g[j] = ("tS", evd)
                    return (strip(w0T, 0, j * 128, NF), group, post)

                stages = [mk_mm1(j, tc, x16_g) for j in range(NFT)]
                run_pipe(ws, stages, pend=(mm1_pend if tc == 0 else None))

                hooks = {}
                if tc + 1 < NCH:
                    xw = ("tP", P.counts["tP"])
                    def later_apply(tc_=tc, xw_=xw):
                        nonlocal x16_g
                        x16_g = emit_apply(tc_ + 1, xw_)
                    hooks.setdefault(3, []).append(later_apply)


                def mk_mm2(dt, s, tc_):
                    psid = 2 + dt % 2
                    ps = (P2, P3)[dt % 2]

                    def group(b, wready):
                        def fn(e, b_=b, ps_=ps, s_=s):
                            ins = None
                            for p, w in ((0, whi[b_]), (1, wlo[b_])):
                                for k in range(NDT):
                                    kk = s_ * 16 + k
                                    ins = e.matmul(ps_[:, 0:TC], w[:, k * 128:(k + 1) * 128],
                                                   D16[:, kk * TC:(kk + 1) * TC],
                                                   start=(s_ == 0 and p == 0 and k == 0),
                                                   stop=(s_ == 3 and p == 1 and k == NDT - 1))
                            return ins
                        return P.emit("tensor", fn,
                                      waits=wready + [d16_g[s * 16 + 15], psg[psid]], inc="tP")

                    def post(gdone):
                        eb = dt % 2
                        ev = P.emit("scalar",
                                    (lambda ps_, dt_, eb_: lambda e: e.activation(
                                        out=E16[eb_][:, :], in_=ps_[:, 0:TC], func=AF.Identity,
                                        bias=b1_sb[:, dt_:dt_ + 1], scale=1.0))(ps, dt, eb),
                                    waits=[("tP", gdone), e_guard[eb]], inc="tS")
                        radd = P.emit("vector",
                                      (lambda dt_, tc2, eb_: lambda e: e.scalar_tensor_tensor(
                                          out=xst(dt_, tc2 * TC, TC), in0=E16[eb_][:, :], scalar=0.5,
                                          in1=xst(dt_, tc2 * TC, TC), op0=ALU.mult, op1=ALU.add))(dt, tc_, eb),
                                      waits=[("tS", ev)], inc="tV")
                        e_guard[eb] = ("tV", radd)
                        psg[psid] = ("tV", radd)
                        state["xg_h"][tc_][dt] = [("tV", radd)]
                        if tc_ == NCH - 1:
                            state["xg"][dt] = [("tV", radd)]
                    return (strip(w1T, s * 2048, dt * 128, D), group,
                            post if s == 3 else None)

                stages = [mk_mm2(dt, s, tc) for dt in range(NDT) for s in range(4)]
                run_pipe(ws, stages, hooks)
            ph.close()

        def emit_attn():
            ph = ExitStack()
            kvt = sb(ph, "at_kvt", [128, NDT * NT], F32)
            q_sb = [sb(ph, f"at_q{b}", [128, TC], F32) for b in range(2)]
            wtmp = sb(ph, "at_wtmp", [128, 2048], F32)
            whi = [sb(ph, f"at_whi{b}", [128, 2048], F16) for b in range(2)]
            wlo = [sb(ph, f"at_wlo{b}", [128, 2048], F16) for b in range(2)]

            ws = WSplit(wtmp, whi, wlo)
            psg = {k: P.now("tV") for k in range(4)}

            ph_a = ExitStack()
            A16h = sb(ph_a, "at_A16h", [128, NDT * TC], F16)
            A16l = sb(ph_a, "at_A16l", [128, NDT * TC], F16)

            st_qpage = {"w": 0}
            vst = {"qc": P.counts["tV"], "pd": P.counts["tV"], "pd2": P.counts["tV"], "ah": 0}
            ln = LNSteps(slots=[kvt[:, 2048 + k * 512: 2048 + (k + 1) * 512]
                                for k in range(4)])

            def emit_a_apply(tc, xwar, scr):
                guards = []
                lh = ll = 0
                for i in range(NDT):
                    a1 = P.emit("vector",
                                (lambda i_, tc_, s0: lambda e: e.tensor_tensor(
                                    out=s0, in0=xst(i_, tc_ * TC, TC),
                                    in1=bcA[:, tc_ * TC:(tc_ + 1) * TC], op=ALU.mult))(i, tc, scr[0]),
                                waits=ln.fin[tc] + list(state["xg_h"][tc][i]) + [xwar, P.now("tV"), P.now("tS")],
                                inc="tV")
                    a2 = P.emit("vector",
                                (lambda i_, tc_, s0, s1: lambda e: e.tensor_tensor(
                                    out=s1, in0=s0,
                                    in1=bcB[:, tc_ * TC:(tc_ + 1) * TC], op=ALU.add))(i, tc, scr[0], scr[1]),
                                waits=[("tV", a1), ("tS", vst["ah"])], inc="tV")
                    lh = P.emit("scalar",
                                (lambda i_, s1: lambda e: e.activation(
                                    out=A16h[:, i_ * TC:(i_ + 1) * TC],
                                    in_=s1,
                                    func=AF.Copy, bias=0.0, scale=1.0))(i, scr[1]),
                                waits=[("tV", a2), P.now("tP")], inc="tS")
                    vst["ah"] = lh
                    ll = P.emit("vector",
                                (lambda i_, s1: lambda e: e.tensor_tensor(
                                    out=A16l[:, i_ * TC:(i_ + 1) * TC],
                                    in0=s1,
                                    in1=A16h[:, i_ * TC:(i_ + 1) * TC],
                                    op=ALU.subtract))(i, scr[1]),
                                waits=[("tS", lh)], inc="tV")
                    guards.append([("tS", lh), ("tV", ll)])
                state["bc_read"][tc] = ll
                return guards

            agc = 0
            rc = 0
            carry_c = 0

            def emit_carry():
                nonlocal carry_c
                agr = 0
                for hh in range(2):
                    agr = P.emit("gpsimd",
                                 (lambda hh_: lambda e: e.dma_start(
                                     out=sq_scr[0:n_cores, :],
                                     in_=ag_out.ap()[0:n_cores, hh_ * 1024:(hh_ + 1) * 1024]))(hh),
                                 waits=[("tG", agc), ("tP", carry_c), ("tS", P.counts["tS"]),
                                        ("tV", P.counts["tV"])],
                                 inc="dA", inc_amt=16)
                    for ii in range(8):
                        i = hh * 8 + ii
                        carry_c = P.emit("tensor",
                                         (lambda i_, ii_: lambda e: e.matmul(
                                             P3[0:128, i_:i_ + 1],
                                             sq_scr[0:n_cores, ii_ * 128:(ii_ + 1) * 128],
                                             csel_sb[0:n_cores, 0:1], start=True, stop=True))(i, ii),
                                         waits=[("dA", agr), ("dG", cst_loaded), ("tV", rc),
                                                psg[3]],
                                         inc="tP")
                cc = P.emit("vector", lambda e: e.tensor_copy(out=carry_sb[:, :], in_=P3[0:128, 0:NDT]),
                            waits=[("tP", carry_c)], inc="tV")
                psg[3] = ("tV", cc)
                return cc

            scan_g = {}
            cc_g = {"cc": 0}

            def emit_scan(i):
                sc = P.emit("vector",
                            (lambda i_: lambda e: e.tensor_tensor_scan(
                                out=kvt[:, i_ * NT: i_ * NT + NT],
                                data0=ones_sc[:, :], data1=kvt[:, i_ * NT: i_ * NT + NT],
                                initial=carry_sb[:, i_:i_ + 1],
                                op0=ALU.mult, op1=ALU.add))(i),
                            waits=[("tV", cc_g["cc"]), ("tV", m1)], inc="tV")
                scan_g[i] = sc
            for i in range(NDT):
                ln.step(i, 0)
            ln.finish_half(0)
            kv_pend = {0: ws.load(strip(wkT, 0, 0, D), 0),
                       1: ws.load(strip(wvT, 0, 0, D), 1)}
            a_guards0 = emit_a_apply(0, ("tP", P.counts["tP"]),
                                     (kvt[:, 0:TC], kvt[:, TC:2 * TC]))
            for i in range(NDT):
                ln.step(i, 1)
            ln.finish_half(1)
            for tc in range(NCH):
                if tc > 0:
                    kv_pend = {0: ws.load(strip(wkT, 0, 0, D), 0),
                               1: ws.load(strip(wvT, 0, 0, D), 1)}
                a_guards = (a_guards0 if tc == 0 else
                            emit_a_apply(tc, ("tP", P.counts["tP"]),
                                         (sq_scr[:, 0:TC], sq_scr[:, TC:2 * TC])))

                def mk_proj(i, pj, tc_, a_g, seq, perk):
                    wT = (wqT, wkT, wvT)[pj]
                    if pj == 1 or pj == 0:
                        psid = seq % 2
                        ps = (P0, P1)[psid]
                    else:
                        psid = 2 + seq % 2
                        ps = (P2, P3)[psid - 2]

                    def group(b, wready):
                        if not perk:
                            def fn(e, b_=b, ps_=ps):
                                ins = None
                                terms = ((whi[b_], A16h), (whi[b_], A16l), (wlo[b_], A16h))
                                for p, (w, a) in enumerate(terms):
                                    for k in range(NDT):
                                        ins = e.matmul(ps_[:, 0:TC], w[:, k * 128:(k + 1) * 128],
                                                       a[:, k * TC:(k + 1) * TC],
                                                       start=(p == 0 and k == 0),
                                                       stop=(p == 2 and k == NDT - 1))
                                return ins
                            return P.emit("tensor", fn,
                                          waits=wready + a_g[NDT - 1] + [psg[psid]], inc="tP")
                        gdone = None
                        terms = ((0, 0), (0, 1), (1, 0))
                        for p, (wsel, asel) in enumerate(terms):
                            for k in range(NDT):
                                wts = (wready + [psg[psid]]) if (p == 0 and k == 0) else []
                                wts = wts + a_g[k]
                                last = (p == 2 and k == NDT - 1)
                                gdone = P.emit("tensor",
                                               (lambda b_, ps_, p_, k_, ws_, as_: lambda e: e.matmul(
                                                   ps_[:, 0:TC],
                                                   (whi, wlo)[ws_][b_][:, k_ * 128:(k_ + 1) * 128],
                                                   (A16h, A16l)[as_][:, k_ * TC:(k_ + 1) * TC],
                                                   start=(p_ == 0 and k_ == 0),
                                                   stop=(p_ == 2 and k_ == NDT - 1)))(
                                                       b, ps, p, k, wsel, asel),
                                               waits=wts,
                                               inc="tP" if last else None)
                        return gdone

                    def post_q(gdone, ps_=None, psid_=None):
                        qc = P.emit("vector",
                                    (lambda ps2: lambda e: e.tensor_copy(
                                        out=q_sb[0][:, :], in_=ps2[:, 0:TC]))(ps_),
                                    waits=[("tP", gdone), ("dQ", st_qpage["w"])], inc="tV")
                        psg[psid_] = ("tV", qc)
                        st_qpage["w"] = P.emit("gpsimd",
                                               (lambda i_, tc2: lambda e: e.dma_start(
                                                   out=hbm_tile_slice(q_page, i_, tc2 * TC, TC),
                                                   in_=q_sb[0][:, :]))(i, tc_),
                                               waits=[("tV", qc)], inc="dQ", inc_amt=16)

                    def post_v(gdone, ps_=None, psid_=None):
                        scr = sq_scr[:, (i % 2) * TC:(i % 2) * TC + TC]
                        kps = (P0, P1)[i % 2]
                        vev = P.emit("vector",
                                     (lambda ps2, scr_: lambda e: e.tensor_copy(
                                         out=scr_, in_=ps2[:, 0:TC]))(ps_, scr),
                                     waits=[("tP", gdone), ("tV", vst["pd2"])], inc="tV")
                        pdone = P.emit("vector",
                                       (lambda i_, tc2, kps_, scr_: lambda e: e.scalar_tensor_tensor(
                                           out=kvt[:, i_ * NT + tc2 * TC: i_ * NT + (tc2 + 1) * TC],
                                           in0=kps_[:, 0:TC], scalar=1.0, in1=scr_,
                                           op0=ALU.mult, op1=ALU.mult,
                                           accum_out=s1acc[:, tc2 * NDT + i_: tc2 * NDT + i_ + 1]))(
                                               i, tc_, kps, scr),
                                       waits=[("tV", vev)], inc="tV")
                        vst["pd2"] = vst["pd"]
                        vst["pd"] = pdone
                        psg[i % 2] = ("tV", pdone)
                        psg[psid_] = ("tV", vev)
                    post0 = (post_q, None, post_v)[pj]
                    post = (None if post0 is None
                            else (lambda g, p2=ps, pi=psid, f=post0: f(g, p2, pi)))
                    return (strip(wT, 0, i * 128, D), group, post)

                kv_stages = [mk_proj(i, pj, tc, a_guards, i, i < 1) for i in range(NDT) for pj in (1, 2)]
                q_stages = [mk_proj(i, 0, tc, a_guards, i, False) for i in range(NDT)]
                run_pipe(ws, kv_stages, pend=kv_pend)
                if tc == NCH - 1:
                    p_all = ("tV", vst["pd"])
                    s1d = P.emit("vector", lambda e: e.tensor_tensor(
                        out=s1T[:, :], in0=s1acc[:, 0:NDT], in1=s1acc[:, NDT:2 * NDT], op=ALU.add),
                        waits=[p_all], inc="tV")
                    tr = P.emit("tensor", lambda e: e.transpose(
                        P3[0:16, 0:128], s1T[:, :], ident_sb[:, :]),
                        waits=[("tV", s1d), psg[3], ("dG", cst_loaded)], inc="tP")
                    rc = P.emit("vector", lambda e: e.tensor_copy(out=s1row[:, :], in_=P3[0:16, 0:128]),
                                waits=[("tP", tr)], inc="tV")
                    agw = P.emit("gpsimd", lambda e: e.dma_start(
                        out=ag_in.ap().rearrange("(i p) -> i p", p=128), in_=s1row[:, :]),
                        waits=[("tV", rc)], inc="dG", inc_amt=16)
                    agc = P.emit("gpsimd", lambda e: e.collective_compute(
                        "AllGather", ALU.bypass,
                        replica_groups=[list(range(n_cores))],
                        ins=[ag_in.ap()],
                        outs=[ag_out.ap()]),
                        waits=[("dG", agw)], inc="tG")
                qhooks = {}
                if tc == NCH - 1:
                    def hk_carry():
                        cc_g["cc"] = emit_carry()
                    qhooks.setdefault(3, []).append(hk_carry)
                    for jj in range(8):
                        qhooks.setdefault(4 + jj, []).append(
                            (lambda j2: lambda: (emit_scan(2 * j2), emit_scan(2 * j2 + 1)))(jj))
                run_pipe(ws, q_stages, qhooks)

            ph_a.close()
            vh_h = sb(ph, "at_vhh", [128, NDT * TC], F16)
            vh_l = sb(ph, "at_vhl", [128, NDT * TC], F16)

            qsem = ("dR", "dR1")
            prev_vm = [P.counts["tV"], P.counts["tV"]]
            qr = {}
            for tc in range(NCH):
                wo_pend = {0: ws.load(strip(woT, 0, 0, D), 0),
                           1: ws.load(strip(woT, 0, 128, D), 1)}
                vh_guard = []

                def emit_qr(i_abs):
                    tcq, iq = divmod(i_abs, NDT)
                    qb = i_abs % 2
                    qr[i_abs] = P.emit("gpsimd",
                                       (lambda iq_, tcq_, qb_: lambda e: e.dma_start(
                                           out=q_sb[qb_][:, :],
                                           in_=hbm_tile_slice(q_page, iq_, tcq_ * TC, TC)))(iq, tcq, qb),
                                       waits=[("dQ", st_qpage["w"]), ("tV", prev_vm[qb])],
                                       inc=qsem[qb], inc_amt=16)
                if tc == 0:
                    emit_qr(0)
                    emit_qr(1)
                for i in range(NDT):
                    i_abs = tc * NDT + i
                    qb = i_abs % 2
                    vm = P.emit("vector",
                                (lambda i_, tc_, qb_: lambda e: e.tensor_tensor(
                                    out=kvt[:, i_ * NT + tc_ * TC: i_ * NT + (tc_ + 1) * TC],
                                    in0=q_sb[qb_][:, :],
                                    in1=kvt[:, i_ * NT + tc_ * TC: i_ * NT + (tc_ + 1) * TC],
                                    op=ALU.mult))(i, tc, qb),
                                waits=[(qsem[qb], qr[i_abs]),
                                       ("tV", scan_g[i])], inc="tV")
                    prev_vm[qb] = vm
                    if i_abs + 2 < 2 * NDT:
                        emit_qr(i_abs + 2)
                    vh1 = P.emit("scalar",
                                 (lambda i_, tc_: lambda e: e.activation(
                                     out=vh_h[:, i_ * TC:(i_ + 1) * TC],
                                     in_=kvt[:, i_ * NT + tc_ * TC: i_ * NT + (tc_ + 1) * TC],
                                     func=AF.Copy, bias=0.0, scale=1.0))(i, tc),
                                 waits=[("tV", vm), P.now("tP")], inc="tS")
                    vh2 = P.emit("vector",
                                 (lambda i_, tc_: lambda e: e.tensor_tensor(
                                     out=vh_l[:, i_ * TC:(i_ + 1) * TC],
                                     in0=kvt[:, i_ * NT + tc_ * TC: i_ * NT + (tc_ + 1) * TC],
                                     in1=vh_h[:, i_ * TC:(i_ + 1) * TC],
                                     op=ALU.subtract))(i, tc),
                                 waits=[("tS", vh1)], inc="tV")
                    vh_guard.append([("tS", vh1), ("tV", vh2)])

                def mk_oproj(f, tc_):
                    psid = f % 2
                    ps = (P0, P1)[psid]

                    def group(b, wready):
                        gdone = None
                        terms = ((0, 0), (0, 1), (1, 0))
                        for p, (wsel, asel) in enumerate(terms):
                            for k in range(NDT):
                                wts = []
                                if p == 0 and k == 0:
                                    wts = wready + [psg[psid]]
                                wts = wts + vh_guard[k]
                                last = (p == 2 and k == NDT - 1)
                                gdone = P.emit("tensor",
                                               (lambda b_, ps_, p_, k_, ws_, as_: lambda e: e.matmul(
                                                   ps_[:, 0:TC],
                                                   (whi, wlo)[ws_][b_][:, k_ * 128:(k_ + 1) * 128],
                                                   (vh_h, vh_l)[as_][:, k_ * TC:(k_ + 1) * TC],
                                                   start=(p_ == 0 and k_ == 0),
                                                   stop=(p_ == 2 and k_ == NDT - 1)))(
                                                       b, ps, p, k, wsel, asel),
                                               waits=wts,
                                               inc="tP" if last else None)
                        return gdone

                    def post(gdone):
                        radd = P.emit("vector",
                                      (lambda f_, tc2, ps_: lambda e: e.scalar_tensor_tensor(
                                          out=xst(f_, tc2 * TC, TC), in0=ps_[:, 0:TC], scalar=DENOM,
                                          in1=xst(f_, tc2 * TC, TC), op0=ALU.mult, op1=ALU.add))(
                                              f, tc_, ps),
                                      waits=[("tP", gdone)], inc="tV")
                        psg[psid] = ("tV", radd)
                        state["xg_h"][tc_][f] = [("tV", radd)]
                        if tc_ == NCH - 1:
                            state["xg"][f] = [("tV", radd)]
                    return (strip(woT, 0, f * 128, D), group, post)

                stages = [mk_oproj(f, tc) for f in range(NDT)]
                run_pipe(ws, stages, pend=wo_pend)
            ph.close()

        emit_ffn("f1")
        emit_attn()
        emit_ffn("f2")

        yw = P.emit("gpsimd", lambda e: e.dma_start(out=hbm_tiles(y), in_=xs[:, :]),
                    waits=[g for gs in state["xg"] for g in gs], inc="dG", inc_amt=16)
        P.emit("gpsimd", lambda e: e.memset(s1row[0:1, 0:1], 0.0), waits=[("dG", yw)])

        with nc.Block() as block:
            @block.sync
            def _(eng):
                P.run("sync", eng)

            @block.scalar
            def _(eng):
                P.run("scalar", eng)

            @block.vector
            def _(eng):
                P.run("vector", eng)

            @block.tensor
            def _(eng):
                P.run("tensor", eng)

            @block.gpsimd
            def _(eng):
                P.run("gpsimd", eng)

    return nc


def make_in_maps(L0_attn_x, L0_attn_Wq, L0_attn_Wk, L0_attn_Wv, L0_attn_Wo,
                 L0_ffn_W0, L0_ffn_bias0, L0_ffn_W1, L0_ffn_bias1, n_cores=8):
    x = np.asarray(L0_attn_x, dtype=np.float32)
    wqT = np.ascontiguousarray(np.asarray(L0_attn_Wq, np.float32).T)
    wkT = np.ascontiguousarray(np.asarray(L0_attn_Wk, np.float32).T)
    wvT = np.ascontiguousarray(np.asarray(L0_attn_Wv, np.float32).T)
    woT = np.ascontiguousarray(np.asarray(L0_attn_Wo, np.float32).T)
    w0T = np.ascontiguousarray(np.asarray(L0_ffn_W0, np.float32).T)
    w1T = np.ascontiguousarray(np.asarray(L0_ffn_W1, np.float32).T)
    b0c = np.ascontiguousarray(np.asarray(L0_ffn_bias0, np.float32).reshape(NFT, 128).T)
    b1c = np.ascontiguousarray(np.asarray(L0_ffn_bias1, np.float32).reshape(NDT, 128).T)
    ident = np.eye(128, dtype=np.float32)
    in_maps = []
    for c in range(n_cores):
        b, h = c // 2, c % 2
        xTc = np.ascontiguousarray(x[b, h * NT:(h + 1) * NT, :].T)
        cs = np.zeros((8, 1), np.float32)
        if c % 2 == 1:
            cs[c - 1, 0] = 1.0
        in_maps.append({
            "xT": xTc, "wqT": wqT, "wkT": wkT, "wvT": wvT, "woT": woT,
            "w0T": w0T, "w1T": w1T, "b0c": b0c, "b1c": b1c,
            "csel": cs, "ident": ident,
        })
    return in_maps


def kernel(L0_attn_x, L0_attn_Wq, L0_attn_Wk, L0_attn_Wv, L0_attn_Wo,
           L0_ffn_W0, L0_ffn_bias0, L0_ffn_W1, L0_ffn_bias1):
    n_cores = 8
    nc = build_program(n_cores)
    in_maps = make_in_maps(L0_attn_x, L0_attn_Wq, L0_attn_Wk, L0_attn_Wv, L0_attn_Wo,
                           L0_ffn_W0, L0_ffn_bias0, L0_ffn_W1, L0_ffn_bias1, n_cores)
    res = run_bass_kernel_spmd(nc, in_maps, core_ids=list(range(n_cores)))
    out = np.empty((4, 2048, 2048), np.float32)
    for c in range(n_cores):
        b, h = c // 2, c % 2
        out[b, h * NT:(h + 1) * NT, :] = np.asarray(res.results[c]["y"]).T
    return out


# version 14
# speedup vs baseline: 1.0283x; 1.0046x over previous
import numpy as np
import concourse.bass as bass
import concourse.mybir as mybir
from concourse.bass_utils import run_bass_kernel_spmd
from contextlib import ExitStack

F32 = mybir.dt.float32
F16 = mybir.dt.float16
AF = mybir.ActivationFunctionType
ALU = mybir.AluOpType

D = 2048
NDT = 16
NT = 1024
TC = 512
NCH = NT // TC
NF = 8192
NFT = 64
DENOM = float(np.float32(np.sqrt(np.float32(2048.0))) / np.float32(2048.0))
EPS = 1e-5
RECIP_D = float(np.float32(1.0) / np.float32(2048.0))

ENGINES = ("sync", "scalar", "vector", "tensor", "gpsimd")


class Plan:

    def __init__(self, nc, sems):
        self.nc = nc
        self.sems = sems
        self.counts = {k: 0 for k in sems}
        self.items = {e: [] for e in ENGINES}
        self.last_wait = {}

    def emit(self, eng, fn, waits=(), inc=None, inc_amt=1):
        wl = []
        for w in waits:
            if w is None:
                continue
            s, v = w
            if v <= 0:
                continue
            assert v <= self.counts[s], (s, v, self.counts[s])
            key = (eng, s)
            if self.last_wait.get(key, -1) >= v:
                continue
            self.last_wait[key] = v
            wl.append((s, v))
        after = None
        if inc is not None:
            self.counts[inc] += inc_amt
            after = self.counts[inc]
        self.items[eng].append((wl, fn, inc, inc_amt))
        return after

    def now(self, sem):
        return (sem, self.counts[sem])

    def run(self, eng_name, eng):
        for wl, fn, inc, inc_amt in self.items[eng_name]:
            for s, v in wl:
                eng.wait_ge(self.sems[s], v)
            ins = fn(eng)
            if inc is not None:
                ins.then_inc(self.sems[inc], inc_amt)


def build_program(n_cores=8):
    nc = bass.Bass("TRN2", target_bir_lowering=False)

    xT = nc.dram_tensor("xT", [D, NT], F32, kind="ExternalInput")
    wqT = nc.dram_tensor("wqT", [D, D], F32, kind="ExternalInput")
    wkT = nc.dram_tensor("wkT", [D, D], F32, kind="ExternalInput")
    wvT = nc.dram_tensor("wvT", [D, D], F32, kind="ExternalInput")
    woT = nc.dram_tensor("woT", [D, D], F32, kind="ExternalInput")
    w0T = nc.dram_tensor("w0T", [D, NF], F32, kind="ExternalInput")
    w1T = nc.dram_tensor("w1T", [NF, D], F32, kind="ExternalInput")
    b0c = nc.dram_tensor("b0c", [128, NFT], F32, kind="ExternalInput")
    b1c = nc.dram_tensor("b1c", [128, NDT], F32, kind="ExternalInput")
    csel = nc.dram_tensor("csel", [8, 1], F32, kind="ExternalInput")
    ident = nc.dram_tensor("ident", [128, 128], F32, kind="ExternalInput")
    y = nc.dram_tensor("y", [D, NT], F32, kind="ExternalOutput")
    q_page = nc.dram_tensor("q_page", [D, NT], F32, kind="Internal")
    ag_in = nc.dram_tensor("ag_in", [D], F32, kind="Internal")
    ag_out = nc.dram_tensor("ag_out", [n_cores, D], F32, kind="Internal")

    def hbm_tiles(t, ncols=NT):
        return bass.AP(t, 0, [[ncols, 128], [128 * ncols, NDT], [1, ncols]])

    def hbm_tile_slice(t, i, c0, w, ncols=NT):
        return bass.AP(t, i * 128 * ncols + c0, [[ncols, 128], [1, w]])

    def strip(t, row0, col0, tot_cols, ncols=128):
        return bass.AP(t, row0 * tot_cols + col0,
                       [[tot_cols, 128], [128 * tot_cols, NDT], [1, ncols]])

    with ExitStack() as st:
        def sb(stk, name, shape, dt):
            return stk.enter_context(nc.sbuf_tensor(name, shape, dt))

        sems = {}
        for s in ("dS0", "dS1", "dG", "dQ", "dR", "dR1", "dA", "tS", "tV", "tP", "tG"):
            sems[s] = st.enter_context(nc.semaphore(s))
        P = Plan(nc, sems)

        xs = sb(st, "xs", [128, NDT * NT], F32)
        bcA = sb(st, "bcA", [128, NT], F32)
        bcB = sb(st, "bcB", [128, NT], F32)
        b0_sb = sb(st, "b0_sb", [128, NFT], F32)
        b1_sb = sb(st, "b1_sb", [128, NDT], F32)
        csel_sb = sb(st, "csel_sb", [8, 1], F32)
        ident_sb = sb(st, "ident_sb", [128, 128], F32)
        ones_sc = sb(st, "ones_sc", [128, NT], F16)
        onescol = sb(st, "onescol", [1, 128], F32)
        onesrow = sb(st, "onesrow", [128, 1], F32)
        statrow = sb(st, "statrow", [1, 512], F32)
        s1acc = sb(st, "s1acc", [128, NDT * NCH], F32)
        s1T = sb(st, "s1T", [128, NDT], F32)
        s1row = sb(st, "s1row", [NDT, 128], F32)
        carry_sb = sb(st, "carry_sb", [128, NDT], F32)
        sq_scr = sb(st, "sq_scr", [128, NT], F32)

        P0 = st.enter_context(nc.psum_tensor("P0", [128, 512], F32))
        P1 = st.enter_context(nc.psum_tensor("P1", [128, 512], F32))
        P2 = st.enter_context(nc.psum_tensor("P2", [128, 512], F32))
        P3 = st.enter_context(nc.psum_tensor("P3", [128, 512], F32))
        PB = st.enter_context(nc.psum_tensor("PB", [128, 1024], F32))
        PR1 = st.enter_context(nc.psum_tensor("PR1", [1, 512], F32))
        PR2 = st.enter_context(nc.psum_tensor("PR2", [1, 512], F32))

        def xst(i, t0=0, w=NT):
            return xs[:, i * NT + t0: i * NT + t0 + w]

        x_loaded = P.emit("gpsimd", lambda e: e.dma_start(out=xs[:, :], in_=hbm_tiles(xT)),
                          inc="dG", inc_amt=16)
        c1 = P.emit("gpsimd", lambda e: e.dma_start(out=b0_sb[:, :], in_=b0c.ap()),
                    waits=[("dG", x_loaded)], inc="dG", inc_amt=16)
        c2 = P.emit("gpsimd", lambda e: e.dma_start(out=b1_sb[:, :], in_=b1c.ap()),
                    waits=[("dG", c1)], inc="dG", inc_amt=16)
        c3 = P.emit("gpsimd", lambda e: e.dma_start(out=csel_sb[:, :], in_=csel.ap()),
                    waits=[("dG", c2)], inc="dG", inc_amt=16)
        cst_loaded = P.emit("gpsimd", lambda e: e.dma_start(out=ident_sb[:, :], in_=ident.ap()),
                            waits=[("dG", c3)], inc="dG", inc_amt=16)
        m1 = P.emit("vector", lambda e: e.memset(ones_sc[:, :], 1.0), inc="tV")
        m2 = P.emit("vector", lambda e: e.memset(onescol[:, :], 1.0), inc="tV")
        m3 = P.emit("vector", lambda e: e.memset(onesrow[:, :], 1.0), inc="tV")

        state = {"xg": [[("dG", x_loaded)] for _ in range(NDT)],
                 "xg_h": [[[("dG", x_loaded)] for _ in range(NDT)] for _ in range(2)],
                 "bc_read": [0, 0], "ln_prev": {}}

        class LNSteps:

            def __init__(self, slots=None):
                self.slots = slots or [sq_scr[:, 0:512], sq_scr[:, 512:1024]]
                self.ns = len(self.slots)
                self.floor = [("tP", P.counts["tP"]), ("tV", P.counts["tV"]),
                              ("tS", P.counts["tS"])]
                self.sum_last = 0
                self.sq_last = 0
                self.sq_mm = [0] * self.ns
                self.sq_op = [0] * self.ns
                self.scr_guard = 0
                self.fin = {}
                prev = state.get("ln_prev", {})
                self.bmm = prev.get("bmm", 0)
                self.pr_guard = prev.get("pr_guard", [])

            def step(self, i, half):
                xgw = list(state["xg_h"][half][i])
                par = (half * NDT + i) % self.ns
                slot = self.slots[par]
                sq = P.emit("scalar",
                            (lambda i_, h, sl: lambda e: e.activation(
                                out=sl,
                                in_=xs[:, i_ * NT + h * 512: i_ * NT + h * 512 + 512],
                                func=AF.Square, bias=0.0, scale=1.0))(i, half, slot),
                            waits=xgw + self.floor + [("tP", self.sq_mm[par]),
                                                      ("tS", self.sq_op[par]),
                                                      ("tV", self.scr_guard)],
                            inc="tS")
                self.sq_op[par] = sq
                first_w = self.pr_guard if i == 0 else []
                self.sum_last = P.emit(
                    "tensor",
                    (lambda i_, h: lambda e: e.matmul(
                        PR1[0:1, 0:512], onesrow[:, :],
                        xs[:, i_ * NT + h * 512: i_ * NT + h * 512 + 512],
                        start=(i_ == 0), stop=(i_ == NDT - 1)))(i, half),
                    waits=xgw + [("tV", m3)] + first_w,
                    inc="tP" if i == NDT - 1 else None)
                self.sq_mm[par] = P.emit(
                    "tensor",
                    (lambda i_, sl: lambda e: e.matmul(
                        PR2[0:1, 0:512], onesrow[:, :], sl,
                        start=(i_ == 0), stop=(i_ == NDT - 1)))(i, slot),
                    waits=[("tS", sq)] + first_w, inc="tP")
                self.sq_last = self.sq_mm[par]

            def finish_half(self, h):
                hs = slice(h * 512, (h + 1) * 512)
                bcr = state["bc_read"][h]
                cp1 = P.emit("vector",
                             (lambda: lambda e: e.tensor_copy(
                                 out=statrow[0:1, 0:512], in_=PR1[0:1, 0:512]))(),
                             waits=[("tP", self.sum_last), ("tP", self.bmm)], inc="tV")
                bmm_a = P.emit("tensor",
                               (lambda h_: lambda e: e.matmul(
                                   PB[:, h_ * 512:(h_ + 1) * 512], onescol[:, :],
                                   statrow[0:1, 0:512], start=True, stop=True))(h),
                               waits=[("tV", cp1), ("tV", m2)], inc="tP")
                meanv = P.emit("vector",
                               (lambda h_: lambda e: e.tensor_scalar(
                                   out=bcB[:, h_ * 512:(h_ + 1) * 512],
                                   in0=PB[:, h_ * 512:(h_ + 1) * 512],
                                   scalar1=RECIP_D, scalar2=None, op0=ALU.mult))(h),
                               waits=[("tP", bmm_a), ("tV", bcr)], inc="tV")
                cp2 = P.emit("vector",
                             (lambda: lambda e: e.tensor_copy(
                                 out=statrow[0:1, 0:512], in_=PR2[0:1, 0:512]))(),
                             waits=[("tP", self.sq_last), ("tP", bmm_a)], inc="tV")
                bmm_b = P.emit("tensor",
                               (lambda h_: lambda e: e.matmul(
                                   PB[:, h_ * 512:(h_ + 1) * 512], onescol[:, :],
                                   statrow[0:1, 0:512], start=True, stop=True))(h),
                               waits=[("tV", cp2), ("tV", meanv)], inc="tP")
                t1 = P.emit("vector",
                            (lambda h_: lambda e: e.tensor_scalar(
                                out=sq_scr[:, 0:512], in0=PB[:, h_ * 512:(h_ + 1) * 512],
                                scalar1=RECIP_D, scalar2=None, op0=ALU.mult))(h),
                            waits=[("tP", bmm_b), ("tP", self.sq_last)], inc="tV")
                t2 = P.emit("vector",
                            (lambda h_: lambda e: e.tensor_tensor(
                                out=bcA[:, h_ * 512:(h_ + 1) * 512],
                                in0=bcB[:, h_ * 512:(h_ + 1) * 512],
                                in1=bcB[:, h_ * 512:(h_ + 1) * 512], op=ALU.mult))(h),
                            waits=[("tV", meanv), ("tV", bcr)], inc="tV")
                t3 = P.emit("vector",
                            (lambda h_: lambda e: e.tensor_tensor(
                                out=bcA[:, h_ * 512:(h_ + 1) * 512], in0=sq_scr[:, 0:512],
                                in1=bcA[:, h_ * 512:(h_ + 1) * 512], op=ALU.subtract))(h),
                            waits=[("tV", t1), ("tV", t2)], inc="tV")
                t3e = P.emit("vector",
                             (lambda h_: lambda e: e.tensor_scalar(
                                 out=bcA[:, h_ * 512:(h_ + 1) * 512],
                                 in0=bcA[:, h_ * 512:(h_ + 1) * 512],
                                 scalar1=EPS, scalar2=None, op0=ALU.add))(h),
                             waits=[("tV", t3)], inc="tV")
                t4 = P.emit("scalar",
                            (lambda h_: lambda e: e.activation(
                                out=bcA[:, h_ * 512:(h_ + 1) * 512],
                                in_=bcA[:, h_ * 512:(h_ + 1) * 512],
                                func=AF.Sqrt, bias=0.0, scale=1.0))(h),
                            waits=[("tV", t3e)], inc="tS")
                t4b = P.emit("vector",
                             (lambda h_: lambda e: e.reciprocal(
                                 out=bcA[:, h_ * 512:(h_ + 1) * 512],
                                 in_=bcA[:, h_ * 512:(h_ + 1) * 512]))(h),
                             waits=[("tS", t4)], inc="tV")
                t5 = P.emit("vector",
                            (lambda h_: lambda e: e.scalar_tensor_tensor(
                                out=bcB[:, h_ * 512:(h_ + 1) * 512],
                                in0=bcB[:, h_ * 512:(h_ + 1) * 512], scalar=-1.0,
                                in1=bcA[:, h_ * 512:(h_ + 1) * 512],
                                op0=ALU.mult, op1=ALU.mult))(h),
                            waits=[("tV", t4b)], inc="tV")
                self.scr_guard = t3
                self.bmm = bmm_b
                self.fin[h] = [("tV", t4b), ("tV", t5)]
                if h == 1:
                    state["ln_prev"] = {"bmm": bmm_b,
                                        "pr_guard": [("tV", cp1), ("tV", cp2)]}

        class WSplit:

            def __init__(self, wtmp, whi, wlo):
                self.wtmp = wtmp if isinstance(wtmp, list) else [wtmp]
                self.nw = len(self.wtmp)
                self.whi, self.wlo = whi, wlo
                self.hi = [P.counts["tS"]] * 2
                self.lo = [P.counts["tV"]] * 2
                self.pe = [P.counts["tP"]] * 2
                self.wh = [P.counts["tS"]] * self.nw
                self.wl = [P.counts["tV"]] * self.nw
                self.seq = 0

            def load(self, src_ap, b):
                w = self.seq % self.nw
                self.seq += 1
                sem = ("dS0", "dS1")[w]
                d = P.emit("sync",
                           (lambda ap_, w_: lambda e: e.dma_start(
                               out=self.wtmp[w_][:, :], in_=ap_))(src_ap, w),
                           waits=[("tS", self.wh[w]), ("tV", self.wl[w])],
                           inc=sem, inc_amt=16)
                self.hi[b] = P.emit("scalar",
                                    (lambda b_, w_: lambda e: e.activation(
                                        out=self.whi[b_][:, :], in_=self.wtmp[w_][:, :],
                                        func=AF.Copy, bias=0.0, scale=1.0))(b, w),
                                    waits=[(sem, d), ("tP", self.pe[b])], inc="tS")
                self.lo[b] = P.emit("vector",
                                    (lambda b_, w_: lambda e: e.tensor_tensor(
                                        out=self.wlo[b_][:, :], in0=self.wtmp[w_][:, :],
                                        in1=self.whi[b_][:, :], op=ALU.subtract))(b, w),
                                    waits=[("tS", self.hi[b]), ("tP", self.pe[b])], inc="tV")
                self.wh[w] = self.hi[b]
                self.wl[w] = self.lo[b]
                return [("tS", self.hi[b]), ("tV", self.lo[b])]

        def run_pipe(ws, stages, hooks=None, pend=None, next_aps=None):
            pend = dict(pend) if pend else {}
            n = len(stages)
            nxt = {}
            for idx in range(min(2, n)):
                if idx not in pend:
                    pend[idx] = ws.load(stages[idx][0], idx % 2)
            for idx in range(n):
                b = idx % 2
                _, group, post = stages[idx]
                gdone = group(b, pend[idx])
                ws.pe[b] = gdone
                if idx + 2 < n:
                    pend[idx + 2] = ws.load(stages[idx + 2][0], (idx + 2) % 2)
                elif next_aps:
                    k = idx + 2 - n
                    if k < len(next_aps):
                        nxt[k] = ws.load(next_aps[k], (idx + 2) % 2)
                if post is not None:
                    post(gdone)
                if hooks and idx in hooks:
                    for fn in hooks[idx]:
                        fn()
            return nxt

        def emit_ffn(tag):
            ph = ExitStack()
            X16 = sb(ph, tag + "_X16", [128, NDT * TC], F16)
            D16 = sb(ph, tag + "_D16", [128, NFT * TC], F16)
            wtmp = [sb(ph, tag + f"_wtmp{w}", [128, 2048], F32) for w in range(2)]
            whi = [sb(ph, tag + f"_whi{b}", [128, 2048], F16) for b in range(2)]
            wlo = [sb(ph, tag + f"_wlo{b}", [128, 2048], F16) for b in range(2)]
            E16 = [sb(ph, tag + f"_E16{b}", [128, TC], F16) for b in range(2)]
            ap_scr = sb(ph, tag + "_apscr", [128, TC], F32)

            ws = WSplit(wtmp, whi, wlo)
            psg = {k: P.now("tV") for k in range(4)}
            e_guard = [P.now("tV"), P.now("tV")]
            ln = LNSteps(slots=[D16[:, k * 1024:(k + 1) * 1024].bitcast(F32)
                                for k in range(4)])

            def emit_apply(tc, xwar):
                guards = []
                done = 0
                for i in range(NDT):
                    a1 = P.emit("vector",
                                (lambda i_, tc_: lambda e: e.tensor_tensor(
                                    out=ap_scr[:, 0:TC], in0=xst(i_, tc_ * TC, TC),
                                    in1=bcA[:, tc_ * TC:(tc_ + 1) * TC], op=ALU.mult))(i, tc),
                                waits=ln.fin[tc] + list(state["xg_h"][tc][i]) + [xwar, P.now("tV")],
                                inc="tV")
                    done = P.emit("vector",
                                  (lambda i_, tc_: lambda e: e.tensor_tensor(
                                      out=X16[:, i_ * TC:(i_ + 1) * TC], in0=ap_scr[:, 0:TC],
                                      in1=bcB[:, tc_ * TC:(tc_ + 1) * TC], op=ALU.add))(i, tc),
                                  waits=[("tV", a1)], inc="tV")
                    guards.append(("tV", done))
                state["bc_read"][tc] = done
                return guards

            for i in range(NDT):
                ln.step(i, 0)
            ln.finish_half(0)
            mm1_pend = {0: ws.load(strip(w0T, 0, 0, NF), 0),
                        1: ws.load(strip(w0T, 0, 128, NF), 1)}
            x16_g = emit_apply(0, ("tP", 0))
            for i in range(NDT):
                ln.step(i, 1)
            ln.finish_half(1)
            for tc in range(NCH):
                d16_g = {}

                def mk_mm1(j, tc_, x16g_):
                    psid = j % 2
                    ps = (P0, P1)[psid]
                    perk = (j == 0)

                    def group(b, wready):
                        if not perk:
                            def fn(e, b_=b, ps_=ps):
                                ins = None
                                for p, w in ((0, whi[b_]), (1, wlo[b_])):
                                    for k in range(NDT):
                                        ins = e.matmul(ps_[:, 0:TC], w[:, k * 128:(k + 1) * 128],
                                                       X16[:, k * TC:(k + 1) * TC],
                                                       start=(p == 0 and k == 0),
                                                       stop=(p == 1 and k == NDT - 1))
                                return ins
                            return P.emit("tensor", fn,
                                          waits=wready + [x16g_[NDT - 1], psg[psid]], inc="tP")
                        gdone = None
                        for p in range(2):
                            for k in range(NDT):
                                wts = (wready + [psg[psid]]) if (p == 0 and k == 0) else []
                                wts = wts + [x16g_[k]]
                                last = (p == 1 and k == NDT - 1)
                                gdone = P.emit("tensor",
                                               (lambda b_, ps_, p_, k_: lambda e: e.matmul(
                                                   ps_[:, 0:TC],
                                                   (whi, wlo)[p_][b_][:, k_ * 128:(k_ + 1) * 128],
                                                   X16[:, k_ * TC:(k_ + 1) * TC],
                                                   start=(p_ == 0 and k_ == 0),
                                                   stop=(p_ == 1 and k_ == NDT - 1)))(b, ps, p, k),
                                               waits=wts, inc="tP" if last else None)
                        return gdone

                    def post(gdone):
                        evd = P.emit("scalar",
                                     (lambda j_, ps_: lambda e: e.activation(
                                         out=D16[:, j_ * TC:(j_ + 1) * TC], in_=ps_[:, 0:TC],
                                         func=AF.Gelu, bias=b0_sb[:, j_:j_ + 1], scale=1.0))(j, ps),
                                     waits=[("tP", gdone), ("dG", cst_loaded)], inc="tS")
                        psg[psid] = ("tS", evd)
                        d16_g[j] = ("tS", evd)
                    return (strip(w0T, 0, j * 128, NF), group, post)

                stages = [mk_mm1(j, tc, x16_g) for j in range(NFT)]
                mm2_pend = run_pipe(ws, stages, pend=(mm1_pend if tc == 0 else None),
                                    next_aps=[strip(w1T, 0, 0, D), strip(w1T, 2048, 0, D)])

                hooks = {}
                if tc + 1 < NCH:
                    xw = ("tP", P.counts["tP"])
                    def later_apply(tc_=tc, xw_=xw):
                        nonlocal x16_g
                        x16_g = emit_apply(tc_ + 1, xw_)
                    hooks.setdefault(3, []).append(later_apply)


                def mk_mm2(dt, s, tc_):
                    psid = 2 + dt % 2
                    ps = (P2, P3)[dt % 2]

                    def group(b, wready):
                        def fn(e, b_=b, ps_=ps, s_=s):
                            ins = None
                            for p, w in ((0, whi[b_]), (1, wlo[b_])):
                                for k in range(NDT):
                                    kk = s_ * 16 + k
                                    ins = e.matmul(ps_[:, 0:TC], w[:, k * 128:(k + 1) * 128],
                                                   D16[:, kk * TC:(kk + 1) * TC],
                                                   start=(s_ == 0 and p == 0 and k == 0),
                                                   stop=(s_ == 3 and p == 1 and k == NDT - 1))
                            return ins
                        return P.emit("tensor", fn,
                                      waits=wready + [d16_g[s * 16 + 15], psg[psid]], inc="tP")

                    def post(gdone):
                        eb = dt % 2
                        ev = P.emit("scalar",
                                    (lambda ps_, dt_, eb_: lambda e: e.activation(
                                        out=E16[eb_][:, :], in_=ps_[:, 0:TC], func=AF.Identity,
                                        bias=b1_sb[:, dt_:dt_ + 1], scale=1.0))(ps, dt, eb),
                                    waits=[("tP", gdone), e_guard[eb]], inc="tS")
                        radd = P.emit("vector",
                                      (lambda dt_, tc2, eb_: lambda e: e.scalar_tensor_tensor(
                                          out=xst(dt_, tc2 * TC, TC), in0=E16[eb_][:, :], scalar=0.5,
                                          in1=xst(dt_, tc2 * TC, TC), op0=ALU.mult, op1=ALU.add))(dt, tc_, eb),
                                      waits=[("tS", ev)], inc="tV")
                        e_guard[eb] = ("tV", radd)
                        psg[psid] = ("tV", radd)
                        state["xg_h"][tc_][dt] = [("tV", radd)]
                        if tc_ == NCH - 1:
                            state["xg"][dt] = [("tV", radd)]
                    return (strip(w1T, s * 2048, dt * 128, D), group,
                            post if s == 3 else None)

                stages = [mk_mm2(dt, s, tc) for dt in range(NDT) for s in range(4)]
                mm1_pend = run_pipe(ws, stages, hooks, pend=mm2_pend,
                                    next_aps=([strip(w0T, 0, 0, NF), strip(w0T, 0, 128, NF)]
                                              if tc + 1 < NCH else None))
            ph.close()

        def emit_attn():
            ph = ExitStack()
            kvt = sb(ph, "at_kvt", [128, NDT * NT], F32)
            q_sb = [sb(ph, f"at_q{b}", [128, TC], F32) for b in range(2)]
            wtmp = sb(ph, "at_wtmp", [128, 2048], F32)
            whi = [sb(ph, f"at_whi{b}", [128, 2048], F16) for b in range(2)]
            wlo = [sb(ph, f"at_wlo{b}", [128, 2048], F16) for b in range(2)]

            ws = WSplit(wtmp, whi, wlo)
            psg = {k: P.now("tV") for k in range(4)}

            ph_a = ExitStack()
            A16h = sb(ph_a, "at_A16h", [128, NDT * TC], F16)
            A16l = sb(ph_a, "at_A16l", [128, NDT * TC], F16)

            st_qpage = {"w": 0}
            vst = {"qc": P.counts["tV"], "pd": P.counts["tV"], "pd2": P.counts["tV"], "ah": 0}
            ln = LNSteps(slots=[kvt[:, 2048 + k * 512: 2048 + (k + 1) * 512]
                                for k in range(4)])

            def emit_a_apply(tc, xwar, scr):
                guards = []
                lh = ll = 0
                for i in range(NDT):
                    a1 = P.emit("vector",
                                (lambda i_, tc_, s0: lambda e: e.tensor_tensor(
                                    out=s0, in0=xst(i_, tc_ * TC, TC),
                                    in1=bcA[:, tc_ * TC:(tc_ + 1) * TC], op=ALU.mult))(i, tc, scr[0]),
                                waits=ln.fin[tc] + list(state["xg_h"][tc][i]) + [xwar, P.now("tV"), P.now("tS")],
                                inc="tV")
                    a2 = P.emit("vector",
                                (lambda i_, tc_, s0, s1: lambda e: e.tensor_tensor(
                                    out=s1, in0=s0,
                                    in1=bcB[:, tc_ * TC:(tc_ + 1) * TC], op=ALU.add))(i, tc, scr[0], scr[1]),
                                waits=[("tV", a1), ("tS", vst["ah"])], inc="tV")
                    lh = P.emit("scalar",
                                (lambda i_, s1: lambda e: e.activation(
                                    out=A16h[:, i_ * TC:(i_ + 1) * TC],
                                    in_=s1,
                                    func=AF.Copy, bias=0.0, scale=1.0))(i, scr[1]),
                                waits=[("tV", a2), P.now("tP")], inc="tS")
                    vst["ah"] = lh
                    ll = P.emit("vector",
                                (lambda i_, s1: lambda e: e.tensor_tensor(
                                    out=A16l[:, i_ * TC:(i_ + 1) * TC],
                                    in0=s1,
                                    in1=A16h[:, i_ * TC:(i_ + 1) * TC],
                                    op=ALU.subtract))(i, scr[1]),
                                waits=[("tS", lh)], inc="tV")
                    guards.append([("tS", lh), ("tV", ll)])
                state["bc_read"][tc] = ll
                return guards

            agc = 0
            rc = 0
            carry_c = 0

            def emit_carry():
                nonlocal carry_c
                agr = 0
                for hh in range(2):
                    agr = P.emit("gpsimd",
                                 (lambda hh_: lambda e: e.dma_start(
                                     out=sq_scr[0:n_cores, :],
                                     in_=ag_out.ap()[0:n_cores, hh_ * 1024:(hh_ + 1) * 1024]))(hh),
                                 waits=[("tG", agc), ("tP", carry_c), ("tS", P.counts["tS"]),
                                        ("tV", P.counts["tV"])],
                                 inc="dA", inc_amt=16)
                    for ii in range(8):
                        i = hh * 8 + ii
                        carry_c = P.emit("tensor",
                                         (lambda i_, ii_: lambda e: e.matmul(
                                             P3[0:128, i_:i_ + 1],
                                             sq_scr[0:n_cores, ii_ * 128:(ii_ + 1) * 128],
                                             csel_sb[0:n_cores, 0:1], start=True, stop=True))(i, ii),
                                         waits=[("dA", agr), ("dG", cst_loaded), ("tV", rc),
                                                psg[3]],
                                         inc="tP")
                cc = P.emit("vector", lambda e: e.tensor_copy(out=carry_sb[:, :], in_=P3[0:128, 0:NDT]),
                            waits=[("tP", carry_c)], inc="tV")
                psg[3] = ("tV", cc)
                return cc

            scan_g = {}
            cc_g = {"cc": 0}

            def emit_scan(i):
                sc = P.emit("vector",
                            (lambda i_: lambda e: e.tensor_tensor_scan(
                                out=kvt[:, i_ * NT: i_ * NT + NT],
                                data0=ones_sc[:, :], data1=kvt[:, i_ * NT: i_ * NT + NT],
                                initial=carry_sb[:, i_:i_ + 1],
                                op0=ALU.mult, op1=ALU.add))(i),
                            waits=[("tV", cc_g["cc"]), ("tV", m1)], inc="tV")
                scan_g[i] = sc
            for i in range(NDT):
                ln.step(i, 0)
            ln.finish_half(0)
            kv_pend = {0: ws.load(strip(wkT, 0, 0, D), 0),
                       1: ws.load(strip(wvT, 0, 0, D), 1)}
            a_guards0 = emit_a_apply(0, ("tP", P.counts["tP"]),
                                     (kvt[:, 0:TC], kvt[:, TC:2 * TC]))
            for i in range(NDT):
                ln.step(i, 1)
            ln.finish_half(1)
            for tc in range(NCH):
                a_guards = (a_guards0 if tc == 0 else
                            emit_a_apply(tc, ("tP", P.counts["tP"]),
                                         (sq_scr[:, 0:TC], sq_scr[:, TC:2 * TC])))

                def mk_proj(i, pj, tc_, a_g, seq, perk):
                    wT = (wqT, wkT, wvT)[pj]
                    if pj == 1 or pj == 0:
                        psid = seq % 2
                        ps = (P0, P1)[psid]
                    else:
                        psid = 2 + seq % 2
                        ps = (P2, P3)[psid - 2]

                    def group(b, wready):
                        if not perk:
                            def fn(e, b_=b, ps_=ps):
                                ins = None
                                terms = ((whi[b_], A16h), (whi[b_], A16l), (wlo[b_], A16h))
                                for p, (w, a) in enumerate(terms):
                                    for k in range(NDT):
                                        ins = e.matmul(ps_[:, 0:TC], w[:, k * 128:(k + 1) * 128],
                                                       a[:, k * TC:(k + 1) * TC],
                                                       start=(p == 0 and k == 0),
                                                       stop=(p == 2 and k == NDT - 1))
                                return ins
                            return P.emit("tensor", fn,
                                          waits=wready + a_g[NDT - 1] + [psg[psid]], inc="tP")
                        gdone = None
                        terms = ((0, 0), (0, 1), (1, 0))
                        for p, (wsel, asel) in enumerate(terms):
                            for k in range(NDT):
                                wts = (wready + [psg[psid]]) if (p == 0 and k == 0) else []
                                wts = wts + a_g[k]
                                last = (p == 2 and k == NDT - 1)
                                gdone = P.emit("tensor",
                                               (lambda b_, ps_, p_, k_, ws_, as_: lambda e: e.matmul(
                                                   ps_[:, 0:TC],
                                                   (whi, wlo)[ws_][b_][:, k_ * 128:(k_ + 1) * 128],
                                                   (A16h, A16l)[as_][:, k_ * TC:(k_ + 1) * TC],
                                                   start=(p_ == 0 and k_ == 0),
                                                   stop=(p_ == 2 and k_ == NDT - 1)))(
                                                       b, ps, p, k, wsel, asel),
                                               waits=wts,
                                               inc="tP" if last else None)
                        return gdone

                    def post_q(gdone, ps_=None, psid_=None):
                        qc = P.emit("vector",
                                    (lambda ps2: lambda e: e.tensor_copy(
                                        out=q_sb[0][:, :], in_=ps2[:, 0:TC]))(ps_),
                                    waits=[("tP", gdone), ("dQ", st_qpage["w"])], inc="tV")
                        psg[psid_] = ("tV", qc)
                        st_qpage["w"] = P.emit("gpsimd",
                                               (lambda i_, tc2: lambda e: e.dma_start(
                                                   out=hbm_tile_slice(q_page, i_, tc2 * TC, TC),
                                                   in_=q_sb[0][:, :]))(i, tc_),
                                               waits=[("tV", qc)], inc="dQ", inc_amt=16)

                    def post_v(gdone, ps_=None, psid_=None):
                        scr = sq_scr[:, (i % 2) * TC:(i % 2) * TC + TC]
                        kps = (P0, P1)[i % 2]
                        vev = P.emit("vector",
                                     (lambda ps2, scr_: lambda e: e.tensor_copy(
                                         out=scr_, in_=ps2[:, 0:TC]))(ps_, scr),
                                     waits=[("tP", gdone), ("tV", vst["pd2"])], inc="tV")
                        pdone = P.emit("vector",
                                       (lambda i_, tc2, kps_, scr_: lambda e: e.scalar_tensor_tensor(
                                           out=kvt[:, i_ * NT + tc2 * TC: i_ * NT + (tc2 + 1) * TC],
                                           in0=kps_[:, 0:TC], scalar=1.0, in1=scr_,
                                           op0=ALU.mult, op1=ALU.mult,
                                           accum_out=s1acc[:, tc2 * NDT + i_: tc2 * NDT + i_ + 1]))(
                                               i, tc_, kps, scr),
                                       waits=[("tV", vev)], inc="tV")
                        vst["pd2"] = vst["pd"]
                        vst["pd"] = pdone
                        psg[i % 2] = ("tV", pdone)
                        psg[psid_] = ("tV", vev)
                    post0 = (post_q, None, post_v)[pj]
                    post = (None if post0 is None
                            else (lambda g, p2=ps, pi=psid, f=post0: f(g, p2, pi)))
                    return (strip(wT, 0, i * 128, D), group, post)

                kv_stages = [mk_proj(i, pj, tc, a_guards, i, i < 1) for i in range(NDT) for pj in (1, 2)]
                q_stages = [mk_proj(i, 0, tc, a_guards, i, False) for i in range(NDT)]
                q_pend = run_pipe(ws, kv_stages, pend=kv_pend,
                                  next_aps=[strip(wqT, 0, 0, D), strip(wqT, 0, 128, D)])
                if tc == NCH - 1:
                    p_all = ("tV", vst["pd"])
                    s1d = P.emit("vector", lambda e: e.tensor_tensor(
                        out=s1T[:, :], in0=s1acc[:, 0:NDT], in1=s1acc[:, NDT:2 * NDT], op=ALU.add),
                        waits=[p_all], inc="tV")
                    tr = P.emit("tensor", lambda e: e.transpose(
                        P3[0:16, 0:128], s1T[:, :], ident_sb[:, :]),
                        waits=[("tV", s1d), psg[3], ("dG", cst_loaded)], inc="tP")
                    rc = P.emit("vector", lambda e: e.tensor_copy(out=s1row[:, :], in_=P3[0:16, 0:128]),
                                waits=[("tP", tr)], inc="tV")
                    agw = P.emit("gpsimd", lambda e: e.dma_start(
                        out=ag_in.ap().rearrange("(i p) -> i p", p=128), in_=s1row[:, :]),
                        waits=[("tV", rc)], inc="dG", inc_amt=16)
                    agc = P.emit("gpsimd", lambda e: e.collective_compute(
                        "AllGather", ALU.bypass,
                        replica_groups=[list(range(n_cores))],
                        ins=[ag_in.ap()],
                        outs=[ag_out.ap()]),
                        waits=[("dG", agw)], inc="tG")
                qhooks = {}
                if tc == NCH - 1:
                    def hk_carry():
                        cc_g["cc"] = emit_carry()
                    qhooks.setdefault(3, []).append(hk_carry)
                    for jj in range(8):
                        qhooks.setdefault(4 + jj, []).append(
                            (lambda j2: lambda: (emit_scan(2 * j2), emit_scan(2 * j2 + 1)))(jj))
                kv_pend = run_pipe(ws, q_stages, qhooks, pend=q_pend,
                                   next_aps=([strip(wkT, 0, 0, D), strip(wvT, 0, 0, D)]
                                             if tc + 1 < NCH else
                                             [strip(woT, 0, 0, D), strip(woT, 0, 128, D)]))

            ph_a.close()
            vh_h = sb(ph, "at_vhh", [128, NDT * TC], F16)
            vh_l = sb(ph, "at_vhl", [128, NDT * TC], F16)

            qsem = ("dR", "dR1")
            prev_vm = [P.counts["tV"], P.counts["tV"]]
            qr = {}
            wo_pend = kv_pend
            for tc in range(NCH):
                vh_guard = []

                def emit_qr(i_abs):
                    tcq, iq = divmod(i_abs, NDT)
                    qb = i_abs % 2
                    qr[i_abs] = P.emit("gpsimd",
                                       (lambda iq_, tcq_, qb_: lambda e: e.dma_start(
                                           out=q_sb[qb_][:, :],
                                           in_=hbm_tile_slice(q_page, iq_, tcq_ * TC, TC)))(iq, tcq, qb),
                                       waits=[("dQ", st_qpage["w"]), ("tV", prev_vm[qb])],
                                       inc=qsem[qb], inc_amt=16)
                if tc == 0:
                    emit_qr(0)
                    emit_qr(1)
                for i in range(NDT):
                    i_abs = tc * NDT + i
                    qb = i_abs % 2
                    vm = P.emit("vector",
                                (lambda i_, tc_, qb_: lambda e: e.tensor_tensor(
                                    out=kvt[:, i_ * NT + tc_ * TC: i_ * NT + (tc_ + 1) * TC],
                                    in0=q_sb[qb_][:, :],
                                    in1=kvt[:, i_ * NT + tc_ * TC: i_ * NT + (tc_ + 1) * TC],
                                    op=ALU.mult))(i, tc, qb),
                                waits=[(qsem[qb], qr[i_abs]),
                                       ("tV", scan_g[i])], inc="tV")
                    prev_vm[qb] = vm
                    if i_abs + 2 < 2 * NDT:
                        emit_qr(i_abs + 2)
                    vh1 = P.emit("scalar",
                                 (lambda i_, tc_: lambda e: e.activation(
                                     out=vh_h[:, i_ * TC:(i_ + 1) * TC],
                                     in_=kvt[:, i_ * NT + tc_ * TC: i_ * NT + (tc_ + 1) * TC],
                                     func=AF.Copy, bias=0.0, scale=1.0))(i, tc),
                                 waits=[("tV", vm), P.now("tP")], inc="tS")
                    vh2 = P.emit("vector",
                                 (lambda i_, tc_: lambda e: e.tensor_tensor(
                                     out=vh_l[:, i_ * TC:(i_ + 1) * TC],
                                     in0=kvt[:, i_ * NT + tc_ * TC: i_ * NT + (tc_ + 1) * TC],
                                     in1=vh_h[:, i_ * TC:(i_ + 1) * TC],
                                     op=ALU.subtract))(i, tc),
                                 waits=[("tS", vh1)], inc="tV")
                    vh_guard.append([("tS", vh1), ("tV", vh2)])

                def mk_oproj(f, tc_):
                    psid = f % 2
                    ps = (P0, P1)[psid]

                    def group(b, wready):
                        gdone = None
                        terms = ((0, 0), (0, 1), (1, 0))
                        for p, (wsel, asel) in enumerate(terms):
                            for k in range(NDT):
                                wts = []
                                if p == 0 and k == 0:
                                    wts = wready + [psg[psid]]
                                wts = wts + vh_guard[k]
                                last = (p == 2 and k == NDT - 1)
                                gdone = P.emit("tensor",
                                               (lambda b_, ps_, p_, k_, ws_, as_: lambda e: e.matmul(
                                                   ps_[:, 0:TC],
                                                   (whi, wlo)[ws_][b_][:, k_ * 128:(k_ + 1) * 128],
                                                   (vh_h, vh_l)[as_][:, k_ * TC:(k_ + 1) * TC],
                                                   start=(p_ == 0 and k_ == 0),
                                                   stop=(p_ == 2 and k_ == NDT - 1)))(
                                                       b, ps, p, k, wsel, asel),
                                               waits=wts,
                                               inc="tP" if last else None)
                        return gdone

                    def post(gdone):
                        radd = P.emit("vector",
                                      (lambda f_, tc2, ps_: lambda e: e.scalar_tensor_tensor(
                                          out=xst(f_, tc2 * TC, TC), in0=ps_[:, 0:TC], scalar=DENOM,
                                          in1=xst(f_, tc2 * TC, TC), op0=ALU.mult, op1=ALU.add))(
                                              f, tc_, ps),
                                      waits=[("tP", gdone)], inc="tV")
                        psg[psid] = ("tV", radd)
                        state["xg_h"][tc_][f] = [("tV", radd)]
                        if tc_ == NCH - 1:
                            state["xg"][f] = [("tV", radd)]
                    return (strip(woT, 0, f * 128, D), group, post)

                stages = [mk_oproj(f, tc) for f in range(NDT)]
                wo_pend = run_pipe(ws, stages, pend=wo_pend,
                                   next_aps=([strip(woT, 0, 0, D), strip(woT, 0, 128, D)]
                                             if tc + 1 < NCH else None))
            ph.close()

        emit_ffn("f1")
        emit_attn()
        emit_ffn("f2")

        yw = P.emit("gpsimd", lambda e: e.dma_start(out=hbm_tiles(y), in_=xs[:, :]),
                    waits=[g for gs in state["xg"] for g in gs], inc="dG", inc_amt=16)
        P.emit("gpsimd", lambda e: e.memset(s1row[0:1, 0:1], 0.0), waits=[("dG", yw)])

        with nc.Block() as block:
            @block.sync
            def _(eng):
                P.run("sync", eng)

            @block.scalar
            def _(eng):
                P.run("scalar", eng)

            @block.vector
            def _(eng):
                P.run("vector", eng)

            @block.tensor
            def _(eng):
                P.run("tensor", eng)

            @block.gpsimd
            def _(eng):
                P.run("gpsimd", eng)

    return nc


def make_in_maps(L0_attn_x, L0_attn_Wq, L0_attn_Wk, L0_attn_Wv, L0_attn_Wo,
                 L0_ffn_W0, L0_ffn_bias0, L0_ffn_W1, L0_ffn_bias1, n_cores=8):
    x = np.asarray(L0_attn_x, dtype=np.float32)
    wqT = np.ascontiguousarray(np.asarray(L0_attn_Wq, np.float32).T)
    wkT = np.ascontiguousarray(np.asarray(L0_attn_Wk, np.float32).T)
    wvT = np.ascontiguousarray(np.asarray(L0_attn_Wv, np.float32).T)
    woT = np.ascontiguousarray(np.asarray(L0_attn_Wo, np.float32).T)
    w0T = np.ascontiguousarray(np.asarray(L0_ffn_W0, np.float32).T)
    w1T = np.ascontiguousarray(np.asarray(L0_ffn_W1, np.float32).T)
    b0c = np.ascontiguousarray(np.asarray(L0_ffn_bias0, np.float32).reshape(NFT, 128).T)
    b1c = np.ascontiguousarray(np.asarray(L0_ffn_bias1, np.float32).reshape(NDT, 128).T)
    ident = np.eye(128, dtype=np.float32)
    in_maps = []
    for c in range(n_cores):
        b, h = c // 2, c % 2
        xTc = np.ascontiguousarray(x[b, h * NT:(h + 1) * NT, :].T)
        cs = np.zeros((8, 1), np.float32)
        if c % 2 == 1:
            cs[c - 1, 0] = 1.0
        in_maps.append({
            "xT": xTc, "wqT": wqT, "wkT": wkT, "wvT": wvT, "woT": woT,
            "w0T": w0T, "w1T": w1T, "b0c": b0c, "b1c": b1c,
            "csel": cs, "ident": ident,
        })
    return in_maps


def kernel(L0_attn_x, L0_attn_Wq, L0_attn_Wk, L0_attn_Wv, L0_attn_Wo,
           L0_ffn_W0, L0_ffn_bias0, L0_ffn_W1, L0_ffn_bias1):
    n_cores = 8
    nc = build_program(n_cores)
    in_maps = make_in_maps(L0_attn_x, L0_attn_Wq, L0_attn_Wk, L0_attn_Wv, L0_attn_Wo,
                           L0_ffn_W0, L0_ffn_bias0, L0_ffn_W1, L0_ffn_bias1, n_cores)
    res = run_bass_kernel_spmd(nc, in_maps, core_ids=list(range(n_cores)))
    out = np.empty((4, 2048, 2048), np.float32)
    for c in range(n_cores):
        b, h = c // 2, c % 2
        out[b, h * NT:(h + 1) * NT, :] = np.asarray(res.results[c]["y"]).T
    return out


# version 15
# speedup vs baseline: 1.0365x; 1.0079x over previous
import numpy as np
import concourse.bass as bass
import concourse.mybir as mybir
from concourse.bass_utils import run_bass_kernel_spmd
from contextlib import ExitStack

F32 = mybir.dt.float32
F16 = mybir.dt.float16
AF = mybir.ActivationFunctionType
ALU = mybir.AluOpType

D = 2048
NDT = 16
NT = 1024
TC = 512
NCH = NT // TC
NF = 8192
NFT = 64
DENOM = float(np.float32(np.sqrt(np.float32(2048.0))) / np.float32(2048.0))
EPS = 1e-5
RECIP_D = float(np.float32(1.0) / np.float32(2048.0))

ENGINES = ("sync", "scalar", "vector", "tensor", "gpsimd")


class Plan:

    def __init__(self, nc, sems):
        self.nc = nc
        self.sems = sems
        self.counts = {k: 0 for k in sems}
        self.items = {e: [] for e in ENGINES}
        self.last_wait = {}

    def emit(self, eng, fn, waits=(), inc=None, inc_amt=1):
        wl = []
        for w in waits:
            if w is None:
                continue
            s, v = w
            if v <= 0:
                continue
            assert v <= self.counts[s], (s, v, self.counts[s])
            key = (eng, s)
            if self.last_wait.get(key, -1) >= v:
                continue
            self.last_wait[key] = v
            wl.append((s, v))
        after = None
        if inc is not None:
            self.counts[inc] += inc_amt
            after = self.counts[inc]
        self.items[eng].append((wl, fn, inc, inc_amt))
        return after

    def now(self, sem):
        return (sem, self.counts[sem])

    def run(self, eng_name, eng):
        for wl, fn, inc, inc_amt in self.items[eng_name]:
            for s, v in wl:
                eng.wait_ge(self.sems[s], v)
            ins = fn(eng)
            if inc is not None:
                ins.then_inc(self.sems[inc], inc_amt)


def build_program(n_cores=8):
    nc = bass.Bass("TRN2", target_bir_lowering=False)

    xT = nc.dram_tensor("xT", [D, NT], F32, kind="ExternalInput")
    wqT = nc.dram_tensor("wqT", [D, D], F32, kind="ExternalInput")
    wkT = nc.dram_tensor("wkT", [D, D], F32, kind="ExternalInput")
    wvT = nc.dram_tensor("wvT", [D, D], F32, kind="ExternalInput")
    woT = nc.dram_tensor("woT", [D, D], F32, kind="ExternalInput")
    w0T = nc.dram_tensor("w0T", [D, NF], F32, kind="ExternalInput")
    w1T = nc.dram_tensor("w1T", [NF, D], F32, kind="ExternalInput")
    b0c = nc.dram_tensor("b0c", [128, NFT], F32, kind="ExternalInput")
    b1c = nc.dram_tensor("b1c", [128, NDT], F32, kind="ExternalInput")
    csel = nc.dram_tensor("csel", [8, 1], F32, kind="ExternalInput")
    ident = nc.dram_tensor("ident", [128, 128], F32, kind="ExternalInput")
    y = nc.dram_tensor("y", [D, NT], F32, kind="ExternalOutput")
    q_page = nc.dram_tensor("q_page", [D, NT], F32, kind="Internal")
    ag_in = nc.dram_tensor("ag_in", [D], F32, kind="Internal")
    ag_out = nc.dram_tensor("ag_out", [n_cores, D], F32, kind="Internal")

    def hbm_tiles(t, ncols=NT):
        return bass.AP(t, 0, [[ncols, 128], [128 * ncols, NDT], [1, ncols]])

    def hbm_tile_slice(t, i, c0, w, ncols=NT):
        return bass.AP(t, i * 128 * ncols + c0, [[ncols, 128], [1, w]])

    def strip(t, row0, col0, tot_cols, ncols=128):
        return bass.AP(t, row0 * tot_cols + col0,
                       [[tot_cols, 128], [128 * tot_cols, NDT], [1, ncols]])

    with ExitStack() as st:
        def sb(stk, name, shape, dt):
            return stk.enter_context(nc.sbuf_tensor(name, shape, dt))

        sems = {}
        for s in ("dS0", "dS1", "dG", "dQ", "dR", "dR1", "dA", "tS", "tV", "tP", "tG"):
            sems[s] = st.enter_context(nc.semaphore(s))
        P = Plan(nc, sems)

        xs = sb(st, "xs", [128, NDT * NT], F32)
        bcA = sb(st, "bcA", [128, NT], F32)
        bcB = sb(st, "bcB", [128, NT], F32)
        b0_sb = sb(st, "b0_sb", [128, NFT], F32)
        b1_sb = sb(st, "b1_sb", [128, NDT], F32)
        csel_sb = sb(st, "csel_sb", [8, 1], F32)
        ident_sb = sb(st, "ident_sb", [128, 128], F32)
        ones_sc = sb(st, "ones_sc", [128, NT], F16)
        onescol = sb(st, "onescol", [1, 128], F32)
        onesrow = sb(st, "onesrow", [128, 1], F32)
        statrow = sb(st, "statrow", [1, 512], F32)
        s1acc = sb(st, "s1acc", [128, NDT * NCH], F32)
        s1T = sb(st, "s1T", [128, NDT], F32)
        s1row = sb(st, "s1row", [NDT, 128], F32)
        carry_sb = sb(st, "carry_sb", [128, NDT], F32)
        sq_scr = sb(st, "sq_scr", [128, NT], F32)

        P0 = st.enter_context(nc.psum_tensor("P0", [128, 512], F32))
        P1 = st.enter_context(nc.psum_tensor("P1", [128, 512], F32))
        P2 = st.enter_context(nc.psum_tensor("P2", [128, 512], F32))
        P3 = st.enter_context(nc.psum_tensor("P3", [128, 512], F32))
        PB = st.enter_context(nc.psum_tensor("PB", [128, 1024], F32))
        PR1 = st.enter_context(nc.psum_tensor("PR1", [1, 512], F32))
        PR2 = st.enter_context(nc.psum_tensor("PR2", [1, 512], F32))

        def xst(i, t0=0, w=NT):
            return xs[:, i * NT + t0: i * NT + t0 + w]

        x_loaded = P.emit("gpsimd", lambda e: e.dma_start(out=xs[:, :], in_=hbm_tiles(xT)),
                          inc="dG", inc_amt=16)
        c1 = P.emit("gpsimd", lambda e: e.dma_start(out=b0_sb[:, :], in_=b0c.ap()),
                    waits=[("dG", x_loaded)], inc="dG", inc_amt=16)
        c2 = P.emit("gpsimd", lambda e: e.dma_start(out=b1_sb[:, :], in_=b1c.ap()),
                    waits=[("dG", c1)], inc="dG", inc_amt=16)
        c3 = P.emit("gpsimd", lambda e: e.dma_start(out=csel_sb[:, :], in_=csel.ap()),
                    waits=[("dG", c2)], inc="dG", inc_amt=16)
        cst_loaded = P.emit("gpsimd", lambda e: e.dma_start(out=ident_sb[:, :], in_=ident.ap()),
                            waits=[("dG", c3)], inc="dG", inc_amt=16)
        m1 = P.emit("vector", lambda e: e.memset(ones_sc[:, :], 1.0), inc="tV")
        m2 = P.emit("vector", lambda e: e.memset(onescol[:, :], 1.0), inc="tV")
        m3 = P.emit("vector", lambda e: e.memset(onesrow[:, :], 1.0), inc="tV")

        state = {"xg": [[("dG", x_loaded)] for _ in range(NDT)],
                 "xg_h": [[[("dG", x_loaded)] for _ in range(NDT)] for _ in range(2)],
                 "bc_read": [0, 0], "ln_prev": {}}

        class LNSteps:

            def __init__(self, slots=None):
                self.slots = slots or [sq_scr[:, 0:512], sq_scr[:, 512:1024]]
                self.ns = len(self.slots)
                self.floor = [("tP", P.counts["tP"]), ("tV", P.counts["tV"]),
                              ("tS", P.counts["tS"])]
                self.sum_last = 0
                self.sq_last = 0
                self.sq_mm = [0] * self.ns
                self.sq_op = [0] * self.ns
                self.scr_guard = 0
                self.fin = {}
                prev = state.get("ln_prev", {})
                self.bmm = prev.get("bmm", 0)
                self.pr_guard = prev.get("pr_guard", [])

            def step(self, i, half):
                xgw = list(state["xg_h"][half][i])
                par = (half * NDT + i) % self.ns
                slot = self.slots[par]
                sq = P.emit("scalar",
                            (lambda i_, h, sl: lambda e: e.activation(
                                out=sl,
                                in_=xs[:, i_ * NT + h * 512: i_ * NT + h * 512 + 512],
                                func=AF.Square, bias=0.0, scale=1.0))(i, half, slot),
                            waits=xgw + self.floor + [("tP", self.sq_mm[par]),
                                                      ("tS", self.sq_op[par]),
                                                      ("tV", self.scr_guard)],
                            inc="tS")
                self.sq_op[par] = sq
                first_w = self.pr_guard if i == 0 else []
                self.sum_last = P.emit(
                    "tensor",
                    (lambda i_, h: lambda e: e.matmul(
                        PR1[0:1, 0:512], onesrow[:, :],
                        xs[:, i_ * NT + h * 512: i_ * NT + h * 512 + 512],
                        start=(i_ == 0), stop=(i_ == NDT - 1)))(i, half),
                    waits=xgw + [("tV", m3)] + first_w,
                    inc="tP" if i == NDT - 1 else None)
                self.sq_mm[par] = P.emit(
                    "tensor",
                    (lambda i_, sl: lambda e: e.matmul(
                        PR2[0:1, 0:512], onesrow[:, :], sl,
                        start=(i_ == 0), stop=(i_ == NDT - 1)))(i, slot),
                    waits=[("tS", sq)] + first_w, inc="tP")
                self.sq_last = self.sq_mm[par]

            def finish_half(self, h):
                hs = slice(h * 512, (h + 1) * 512)
                bcr = state["bc_read"][h]
                cp1 = P.emit("vector",
                             (lambda: lambda e: e.tensor_copy(
                                 out=statrow[0:1, 0:512], in_=PR1[0:1, 0:512]))(),
                             waits=[("tP", self.sum_last), ("tP", self.bmm)], inc="tV")
                bmm_a = P.emit("tensor",
                               (lambda h_: lambda e: e.matmul(
                                   PB[:, h_ * 512:(h_ + 1) * 512], onescol[:, :],
                                   statrow[0:1, 0:512], start=True, stop=True))(h),
                               waits=[("tV", cp1), ("tV", m2)], inc="tP")
                meanv = P.emit("vector",
                               (lambda h_: lambda e: e.tensor_scalar(
                                   out=bcB[:, h_ * 512:(h_ + 1) * 512],
                                   in0=PB[:, h_ * 512:(h_ + 1) * 512],
                                   scalar1=RECIP_D, scalar2=None, op0=ALU.mult))(h),
                               waits=[("tP", bmm_a), ("tV", bcr)], inc="tV")
                cp2 = P.emit("vector",
                             (lambda: lambda e: e.tensor_copy(
                                 out=statrow[0:1, 0:512], in_=PR2[0:1, 0:512]))(),
                             waits=[("tP", self.sq_last), ("tP", bmm_a)], inc="tV")
                bmm_b = P.emit("tensor",
                               (lambda h_: lambda e: e.matmul(
                                   PB[:, h_ * 512:(h_ + 1) * 512], onescol[:, :],
                                   statrow[0:1, 0:512], start=True, stop=True))(h),
                               waits=[("tV", cp2), ("tV", meanv)], inc="tP")
                t1 = P.emit("vector",
                            (lambda h_: lambda e: e.tensor_scalar(
                                out=sq_scr[:, 0:512], in0=PB[:, h_ * 512:(h_ + 1) * 512],
                                scalar1=RECIP_D, scalar2=None, op0=ALU.mult))(h),
                            waits=[("tP", bmm_b), ("tP", self.sq_last)], inc="tV")
                t2 = P.emit("vector",
                            (lambda h_: lambda e: e.tensor_tensor(
                                out=bcA[:, h_ * 512:(h_ + 1) * 512],
                                in0=bcB[:, h_ * 512:(h_ + 1) * 512],
                                in1=bcB[:, h_ * 512:(h_ + 1) * 512], op=ALU.mult))(h),
                            waits=[("tV", meanv), ("tV", bcr)], inc="tV")
                t3 = P.emit("vector",
                            (lambda h_: lambda e: e.tensor_tensor(
                                out=bcA[:, h_ * 512:(h_ + 1) * 512], in0=sq_scr[:, 0:512],
                                in1=bcA[:, h_ * 512:(h_ + 1) * 512], op=ALU.subtract))(h),
                            waits=[("tV", t1), ("tV", t2)], inc="tV")
                t3e = P.emit("vector",
                             (lambda h_: lambda e: e.tensor_scalar(
                                 out=bcA[:, h_ * 512:(h_ + 1) * 512],
                                 in0=bcA[:, h_ * 512:(h_ + 1) * 512],
                                 scalar1=EPS, scalar2=None, op0=ALU.add))(h),
                             waits=[("tV", t3)], inc="tV")
                t4 = P.emit("scalar",
                            (lambda h_: lambda e: e.activation(
                                out=bcA[:, h_ * 512:(h_ + 1) * 512],
                                in_=bcA[:, h_ * 512:(h_ + 1) * 512],
                                func=AF.Sqrt, bias=0.0, scale=1.0))(h),
                            waits=[("tV", t3e)], inc="tS")
                t4b = P.emit("vector",
                             (lambda h_: lambda e: e.reciprocal(
                                 out=bcA[:, h_ * 512:(h_ + 1) * 512],
                                 in_=bcA[:, h_ * 512:(h_ + 1) * 512]))(h),
                             waits=[("tS", t4)], inc="tV")
                t5 = P.emit("vector",
                            (lambda h_: lambda e: e.scalar_tensor_tensor(
                                out=bcB[:, h_ * 512:(h_ + 1) * 512],
                                in0=bcB[:, h_ * 512:(h_ + 1) * 512], scalar=-1.0,
                                in1=bcA[:, h_ * 512:(h_ + 1) * 512],
                                op0=ALU.mult, op1=ALU.mult))(h),
                            waits=[("tV", t4b)], inc="tV")
                self.scr_guard = t3
                self.bmm = bmm_b
                self.fin[h] = [("tV", t4b), ("tV", t5)]
                if h == 1:
                    state["ln_prev"] = {"bmm": bmm_b,
                                        "pr_guard": [("tV", cp1), ("tV", cp2)]}

        class WSplit:

            def __init__(self, wtmp, whi, wlo):
                self.wtmp = wtmp if isinstance(wtmp, list) else [wtmp]
                self.nw = len(self.wtmp)
                self.whi, self.wlo = whi, wlo
                self.nb = len(whi)
                self.hi = [P.counts["tS"]] * self.nb
                self.lo = [P.counts["tV"]] * self.nb
                self.pe = [P.counts["tP"]] * self.nb
                self.wh = [P.counts["tS"]] * self.nw
                self.wl = [P.counts["tV"]] * self.nw
                self.seq = 0
                self.bseq = 0

            def load(self, src_ap, b):
                w = self.seq % self.nw
                self.seq += 1
                sem = ("dS0", "dS1")[w]
                d = P.emit("sync",
                           (lambda ap_, w_: lambda e: e.dma_start(
                               out=self.wtmp[w_][:, :], in_=ap_))(src_ap, w),
                           waits=[("tS", self.wh[w]), ("tV", self.wl[w])],
                           inc=sem, inc_amt=16)
                self.hi[b] = P.emit("scalar",
                                    (lambda b_, w_: lambda e: e.activation(
                                        out=self.whi[b_][:, :], in_=self.wtmp[w_][:, :],
                                        func=AF.Copy, bias=0.0, scale=1.0))(b, w),
                                    waits=[(sem, d), ("tP", self.pe[b])], inc="tS")
                self.lo[b] = P.emit("vector",
                                    (lambda b_, w_: lambda e: e.tensor_tensor(
                                        out=self.wlo[b_][:, :], in0=self.wtmp[w_][:, :],
                                        in1=self.whi[b_][:, :], op=ALU.subtract))(b, w),
                                    waits=[("tS", self.hi[b]), ("tP", self.pe[b])], inc="tV")
                self.wh[w] = self.hi[b]
                self.wl[w] = self.lo[b]
                return [("tS", self.hi[b]), ("tV", self.lo[b])]

        def run_pipe(ws, stages, hooks=None, pend=None, next_aps=None):
            pend = dict(pend) if pend else {}
            n = len(stages)
            nb = ws.nb
            b0 = ws.bseq
            nxt = {}
            for idx in range(min(2, n)):
                if idx not in pend:
                    pend[idx] = ws.load(stages[idx][0], (b0 + idx) % nb)
            for idx in range(n):
                b = (b0 + idx) % nb
                _, group, post = stages[idx]
                gdone = group(b, pend[idx])
                ws.pe[b] = gdone
                if idx + 2 < n:
                    pend[idx + 2] = ws.load(stages[idx + 2][0], (b0 + idx + 2) % nb)
                elif next_aps:
                    k = idx + 2 - n
                    if k < len(next_aps):
                        nxt[k] = ws.load(next_aps[k], (b0 + idx + 2) % nb)
                if post is not None:
                    post(gdone)
                if hooks and idx in hooks:
                    for fn in hooks[idx]:
                        fn()
            ws.bseq = (b0 + n) % nb
            return nxt

        def emit_ffn(tag):
            ph = ExitStack()
            X16 = sb(ph, tag + "_X16", [128, NDT * TC], F16)
            D16 = sb(ph, tag + "_D16", [128, NFT * TC], F16)
            wtmp = [sb(ph, tag + f"_wtmp{w}", [128, 2048], F32) for w in range(2)]
            whi = [sb(ph, tag + f"_whi{b}", [128, 2048], F16) for b in range(3)]
            wlo = [sb(ph, tag + f"_wlo{b}", [128, 2048], F16) for b in range(3)]
            E16 = [sb(ph, tag + f"_E16{b}", [128, TC], F16) for b in range(2)]
            ap_scr = sb(ph, tag + "_apscr", [128, TC], F32)

            ws = WSplit(wtmp, whi, wlo)
            psg = {k: P.now("tV") for k in range(4)}
            e_guard = [P.now("tV"), P.now("tV")]
            ln = LNSteps(slots=[D16[:, k * 1024:(k + 1) * 1024].bitcast(F32)
                                for k in range(4)])

            def emit_apply(tc, xwar):
                guards = []
                done = 0
                for i in range(NDT):
                    a1 = P.emit("vector",
                                (lambda i_, tc_: lambda e: e.tensor_tensor(
                                    out=ap_scr[:, 0:TC], in0=xst(i_, tc_ * TC, TC),
                                    in1=bcA[:, tc_ * TC:(tc_ + 1) * TC], op=ALU.mult))(i, tc),
                                waits=ln.fin[tc] + list(state["xg_h"][tc][i]) + [xwar, P.now("tV")],
                                inc="tV")
                    done = P.emit("vector",
                                  (lambda i_, tc_: lambda e: e.tensor_tensor(
                                      out=X16[:, i_ * TC:(i_ + 1) * TC], in0=ap_scr[:, 0:TC],
                                      in1=bcB[:, tc_ * TC:(tc_ + 1) * TC], op=ALU.add))(i, tc),
                                  waits=[("tV", a1)], inc="tV")
                    guards.append(("tV", done))
                state["bc_read"][tc] = done
                return guards

            for i in range(NDT):
                ln.step(i, 0)
            ln.finish_half(0)
            mm1_pend = {0: ws.load(strip(w0T, 0, 0, NF), 0),
                        1: ws.load(strip(w0T, 0, 128, NF), 1)}
            x16_g = emit_apply(0, ("tP", 0))
            for i in range(NDT):
                ln.step(i, 1)
            ln.finish_half(1)
            for tc in range(NCH):
                d16_g = {}

                def mk_mm1(j, tc_, x16g_):
                    psid = j % 2
                    ps = (P0, P1)[psid]
                    perk = (j == 0)

                    def group(b, wready):
                        if not perk:
                            def fn(e, b_=b, ps_=ps):
                                ins = None
                                for p, w in ((0, whi[b_]), (1, wlo[b_])):
                                    for k in range(NDT):
                                        ins = e.matmul(ps_[:, 0:TC], w[:, k * 128:(k + 1) * 128],
                                                       X16[:, k * TC:(k + 1) * TC],
                                                       start=(p == 0 and k == 0),
                                                       stop=(p == 1 and k == NDT - 1))
                                return ins
                            return P.emit("tensor", fn,
                                          waits=wready + [x16g_[NDT - 1], psg[psid]], inc="tP")
                        gdone = None
                        for p in range(2):
                            for k in range(NDT):
                                wts = (wready + [psg[psid]]) if (p == 0 and k == 0) else []
                                wts = wts + [x16g_[k]]
                                last = (p == 1 and k == NDT - 1)
                                gdone = P.emit("tensor",
                                               (lambda b_, ps_, p_, k_: lambda e: e.matmul(
                                                   ps_[:, 0:TC],
                                                   (whi, wlo)[p_][b_][:, k_ * 128:(k_ + 1) * 128],
                                                   X16[:, k_ * TC:(k_ + 1) * TC],
                                                   start=(p_ == 0 and k_ == 0),
                                                   stop=(p_ == 1 and k_ == NDT - 1)))(b, ps, p, k),
                                               waits=wts, inc="tP" if last else None)
                        return gdone

                    def post(gdone):
                        evd = P.emit("scalar",
                                     (lambda j_, ps_: lambda e: e.activation(
                                         out=D16[:, j_ * TC:(j_ + 1) * TC], in_=ps_[:, 0:TC],
                                         func=AF.Gelu, bias=b0_sb[:, j_:j_ + 1], scale=1.0))(j, ps),
                                     waits=[("tP", gdone), ("dG", cst_loaded)], inc="tS")
                        psg[psid] = ("tS", evd)
                        d16_g[j] = ("tS", evd)
                    return (strip(w0T, 0, j * 128, NF), group, post)

                stages = [mk_mm1(j, tc, x16_g) for j in range(NFT)]
                mm2_pend = run_pipe(ws, stages, pend=(mm1_pend if tc == 0 else None),
                                    next_aps=[strip(w1T, 0, 0, D), strip(w1T, 2048, 0, D)])

                hooks = {}
                if tc + 1 < NCH:
                    xw = ("tP", P.counts["tP"])
                    def later_apply(tc_=tc, xw_=xw):
                        nonlocal x16_g
                        x16_g = emit_apply(tc_ + 1, xw_)
                    hooks.setdefault(3, []).append(later_apply)


                def mk_mm2(dt, s, tc_):
                    psid = 2 + dt % 2
                    ps = (P2, P3)[dt % 2]

                    def group(b, wready):
                        def fn(e, b_=b, ps_=ps, s_=s):
                            ins = None
                            for p, w in ((0, whi[b_]), (1, wlo[b_])):
                                for k in range(NDT):
                                    kk = s_ * 16 + k
                                    ins = e.matmul(ps_[:, 0:TC], w[:, k * 128:(k + 1) * 128],
                                                   D16[:, kk * TC:(kk + 1) * TC],
                                                   start=(s_ == 0 and p == 0 and k == 0),
                                                   stop=(s_ == 3 and p == 1 and k == NDT - 1))
                            return ins
                        return P.emit("tensor", fn,
                                      waits=wready + [d16_g[s * 16 + 15], psg[psid]], inc="tP")

                    def post(gdone):
                        eb = dt % 2
                        ev = P.emit("scalar",
                                    (lambda ps_, dt_, eb_: lambda e: e.activation(
                                        out=E16[eb_][:, :], in_=ps_[:, 0:TC], func=AF.Identity,
                                        bias=b1_sb[:, dt_:dt_ + 1], scale=1.0))(ps, dt, eb),
                                    waits=[("tP", gdone), e_guard[eb]], inc="tS")
                        radd = P.emit("vector",
                                      (lambda dt_, tc2, eb_: lambda e: e.scalar_tensor_tensor(
                                          out=xst(dt_, tc2 * TC, TC), in0=E16[eb_][:, :], scalar=0.5,
                                          in1=xst(dt_, tc2 * TC, TC), op0=ALU.mult, op1=ALU.add))(dt, tc_, eb),
                                      waits=[("tS", ev)], inc="tV")
                        e_guard[eb] = ("tV", radd)
                        psg[psid] = ("tV", radd)
                        state["xg_h"][tc_][dt] = [("tV", radd)]
                        if tc_ == NCH - 1:
                            state["xg"][dt] = [("tV", radd)]
                    return (strip(w1T, s * 2048, dt * 128, D), group,
                            post if s == 3 else None)

                stages = [mk_mm2(dt, s, tc) for dt in range(NDT) for s in range(4)]
                mm1_pend = run_pipe(ws, stages, hooks, pend=mm2_pend,
                                    next_aps=([strip(w0T, 0, 0, NF), strip(w0T, 0, 128, NF)]
                                              if tc + 1 < NCH else None))
            ph.close()

        def emit_attn():
            ph = ExitStack()
            kvt = sb(ph, "at_kvt", [128, NDT * NT], F32)
            q_sb = [sb(ph, f"at_q{b}", [128, TC], F32) for b in range(2)]
            wtmp = sb(ph, "at_wtmp", [128, 2048], F32)
            whi = [sb(ph, f"at_whi{b}", [128, 2048], F16) for b in range(2)]
            wlo = [sb(ph, f"at_wlo{b}", [128, 2048], F16) for b in range(2)]

            ws = WSplit(wtmp, whi, wlo)
            psg = {k: P.now("tV") for k in range(4)}

            ph_a = ExitStack()
            A16h = sb(ph_a, "at_A16h", [128, NDT * TC], F16)
            A16l = sb(ph_a, "at_A16l", [128, NDT * TC], F16)

            st_qpage = {"w": 0}
            vst = {"qc": P.counts["tV"], "pd": P.counts["tV"], "pd2": P.counts["tV"], "ah": 0}
            ln = LNSteps(slots=[kvt[:, 2048 + k * 512: 2048 + (k + 1) * 512]
                                for k in range(4)])

            def emit_a_apply(tc, xwar, scr):
                guards = []
                lh = ll = 0
                for i in range(NDT):
                    a1 = P.emit("vector",
                                (lambda i_, tc_, s0: lambda e: e.tensor_tensor(
                                    out=s0, in0=xst(i_, tc_ * TC, TC),
                                    in1=bcA[:, tc_ * TC:(tc_ + 1) * TC], op=ALU.mult))(i, tc, scr[0]),
                                waits=ln.fin[tc] + list(state["xg_h"][tc][i]) + [xwar, P.now("tV"), P.now("tS")],
                                inc="tV")
                    a2 = P.emit("vector",
                                (lambda i_, tc_, s0, s1: lambda e: e.tensor_tensor(
                                    out=s1, in0=s0,
                                    in1=bcB[:, tc_ * TC:(tc_ + 1) * TC], op=ALU.add))(i, tc, scr[0], scr[1]),
                                waits=[("tV", a1), ("tS", vst["ah"])], inc="tV")
                    lh = P.emit("scalar",
                                (lambda i_, s1: lambda e: e.activation(
                                    out=A16h[:, i_ * TC:(i_ + 1) * TC],
                                    in_=s1,
                                    func=AF.Copy, bias=0.0, scale=1.0))(i, scr[1]),
                                waits=[("tV", a2), P.now("tP")], inc="tS")
                    vst["ah"] = lh
                    ll = P.emit("vector",
                                (lambda i_, s1: lambda e: e.tensor_tensor(
                                    out=A16l[:, i_ * TC:(i_ + 1) * TC],
                                    in0=s1,
                                    in1=A16h[:, i_ * TC:(i_ + 1) * TC],
                                    op=ALU.subtract))(i, scr[1]),
                                waits=[("tS", lh)], inc="tV")
                    guards.append([("tS", lh), ("tV", ll)])
                state["bc_read"][tc] = ll
                return guards

            agc = 0
            rc = 0
            carry_c = 0

            def emit_carry():
                nonlocal carry_c
                agr = 0
                for hh in range(2):
                    agr = P.emit("gpsimd",
                                 (lambda hh_: lambda e: e.dma_start(
                                     out=sq_scr[0:n_cores, :],
                                     in_=ag_out.ap()[0:n_cores, hh_ * 1024:(hh_ + 1) * 1024]))(hh),
                                 waits=[("tG", agc), ("tP", carry_c), ("tS", P.counts["tS"]),
                                        ("tV", P.counts["tV"])],
                                 inc="dA", inc_amt=16)
                    for ii in range(8):
                        i = hh * 8 + ii
                        carry_c = P.emit("tensor",
                                         (lambda i_, ii_: lambda e: e.matmul(
                                             P3[0:128, i_:i_ + 1],
                                             sq_scr[0:n_cores, ii_ * 128:(ii_ + 1) * 128],
                                             csel_sb[0:n_cores, 0:1], start=True, stop=True))(i, ii),
                                         waits=[("dA", agr), ("dG", cst_loaded), ("tV", rc),
                                                psg[3]],
                                         inc="tP")
                cc = P.emit("vector", lambda e: e.tensor_copy(out=carry_sb[:, :], in_=P3[0:128, 0:NDT]),
                            waits=[("tP", carry_c)], inc="tV")
                psg[3] = ("tV", cc)
                return cc

            scan_g = {}
            cc_g = {"cc": 0}

            def emit_scan(i):
                sc = P.emit("vector",
                            (lambda i_: lambda e: e.tensor_tensor_scan(
                                out=kvt[:, i_ * NT: i_ * NT + NT],
                                data0=ones_sc[:, :], data1=kvt[:, i_ * NT: i_ * NT + NT],
                                initial=carry_sb[:, i_:i_ + 1],
                                op0=ALU.mult, op1=ALU.add))(i),
                            waits=[("tV", cc_g["cc"]), ("tV", m1)], inc="tV")
                scan_g[i] = sc
            for i in range(NDT):
                ln.step(i, 0)
            ln.finish_half(0)
            kv_pend = {0: ws.load(strip(wkT, 0, 0, D), 0),
                       1: ws.load(strip(wvT, 0, 0, D), 1)}
            a_guards0 = emit_a_apply(0, ("tP", P.counts["tP"]),
                                     (kvt[:, 0:TC], kvt[:, TC:2 * TC]))
            for i in range(NDT):
                ln.step(i, 1)
            ln.finish_half(1)
            for tc in range(NCH):
                a_guards = (a_guards0 if tc == 0 else
                            emit_a_apply(tc, ("tP", P.counts["tP"]),
                                         (sq_scr[:, 0:TC], sq_scr[:, TC:2 * TC])))

                def mk_proj(i, pj, tc_, a_g, seq, perk):
                    wT = (wqT, wkT, wvT)[pj]
                    if pj == 1 or pj == 0:
                        psid = seq % 2
                        ps = (P0, P1)[psid]
                    else:
                        psid = 2 + seq % 2
                        ps = (P2, P3)[psid - 2]

                    def group(b, wready):
                        if not perk:
                            def fn(e, b_=b, ps_=ps):
                                ins = None
                                terms = ((whi[b_], A16h), (whi[b_], A16l), (wlo[b_], A16h))
                                for p, (w, a) in enumerate(terms):
                                    for k in range(NDT):
                                        ins = e.matmul(ps_[:, 0:TC], w[:, k * 128:(k + 1) * 128],
                                                       a[:, k * TC:(k + 1) * TC],
                                                       start=(p == 0 and k == 0),
                                                       stop=(p == 2 and k == NDT - 1))
                                return ins
                            return P.emit("tensor", fn,
                                          waits=wready + a_g[NDT - 1] + [psg[psid]], inc="tP")
                        gdone = None
                        terms = ((0, 0), (0, 1), (1, 0))
                        for p, (wsel, asel) in enumerate(terms):
                            for k in range(NDT):
                                wts = (wready + [psg[psid]]) if (p == 0 and k == 0) else []
                                wts = wts + a_g[k]
                                last = (p == 2 and k == NDT - 1)
                                gdone = P.emit("tensor",
                                               (lambda b_, ps_, p_, k_, ws_, as_: lambda e: e.matmul(
                                                   ps_[:, 0:TC],
                                                   (whi, wlo)[ws_][b_][:, k_ * 128:(k_ + 1) * 128],
                                                   (A16h, A16l)[as_][:, k_ * TC:(k_ + 1) * TC],
                                                   start=(p_ == 0 and k_ == 0),
                                                   stop=(p_ == 2 and k_ == NDT - 1)))(
                                                       b, ps, p, k, wsel, asel),
                                               waits=wts,
                                               inc="tP" if last else None)
                        return gdone

                    def post_q(gdone, ps_=None, psid_=None):
                        qc = P.emit("vector",
                                    (lambda ps2: lambda e: e.tensor_copy(
                                        out=q_sb[0][:, :], in_=ps2[:, 0:TC]))(ps_),
                                    waits=[("tP", gdone), ("dQ", st_qpage["w"])], inc="tV")
                        psg[psid_] = ("tV", qc)
                        st_qpage["w"] = P.emit("gpsimd",
                                               (lambda i_, tc2: lambda e: e.dma_start(
                                                   out=hbm_tile_slice(q_page, i_, tc2 * TC, TC),
                                                   in_=q_sb[0][:, :]))(i, tc_),
                                               waits=[("tV", qc)], inc="dQ", inc_amt=16)

                    def post_v(gdone, ps_=None, psid_=None):
                        scr = sq_scr[:, (i % 2) * TC:(i % 2) * TC + TC]
                        kps = (P0, P1)[i % 2]
                        vev = P.emit("vector",
                                     (lambda ps2, scr_: lambda e: e.tensor_copy(
                                         out=scr_, in_=ps2[:, 0:TC]))(ps_, scr),
                                     waits=[("tP", gdone), ("tV", vst["pd2"])], inc="tV")
                        pdone = P.emit("vector",
                                       (lambda i_, tc2, kps_, scr_: lambda e: e.scalar_tensor_tensor(
                                           out=kvt[:, i_ * NT + tc2 * TC: i_ * NT + (tc2 + 1) * TC],
                                           in0=kps_[:, 0:TC], scalar=1.0, in1=scr_,
                                           op0=ALU.mult, op1=ALU.mult,
                                           accum_out=s1acc[:, tc2 * NDT + i_: tc2 * NDT + i_ + 1]))(
                                               i, tc_, kps, scr),
                                       waits=[("tV", vev)], inc="tV")
                        vst["pd2"] = vst["pd"]
                        vst["pd"] = pdone
                        psg[i % 2] = ("tV", pdone)
                        psg[psid_] = ("tV", vev)
                    post0 = (post_q, None, post_v)[pj]
                    post = (None if post0 is None
                            else (lambda g, p2=ps, pi=psid, f=post0: f(g, p2, pi)))
                    return (strip(wT, 0, i * 128, D), group, post)

                kv_stages = [mk_proj(i, pj, tc, a_guards, i, i < 1) for i in range(NDT) for pj in (1, 2)]
                q_stages = [mk_proj(i, 0, tc, a_guards, i, False) for i in range(NDT)]
                q_pend = run_pipe(ws, kv_stages, pend=kv_pend,
                                  next_aps=[strip(wqT, 0, 0, D), strip(wqT, 0, 128, D)])
                if tc == NCH - 1:
                    p_all = ("tV", vst["pd"])
                    s1d = P.emit("vector", lambda e: e.tensor_tensor(
                        out=s1T[:, :], in0=s1acc[:, 0:NDT], in1=s1acc[:, NDT:2 * NDT], op=ALU.add),
                        waits=[p_all], inc="tV")
                    tr = P.emit("tensor", lambda e: e.transpose(
                        P3[0:16, 0:128], s1T[:, :], ident_sb[:, :]),
                        waits=[("tV", s1d), psg[3], ("dG", cst_loaded)], inc="tP")
                    rc = P.emit("vector", lambda e: e.tensor_copy(out=s1row[:, :], in_=P3[0:16, 0:128]),
                                waits=[("tP", tr)], inc="tV")
                    agw = P.emit("gpsimd", lambda e: e.dma_start(
                        out=ag_in.ap().rearrange("(i p) -> i p", p=128), in_=s1row[:, :]),
                        waits=[("tV", rc)], inc="dG", inc_amt=16)
                    agc = P.emit("gpsimd", lambda e: e.collective_compute(
                        "AllGather", ALU.bypass,
                        replica_groups=[list(range(n_cores))],
                        ins=[ag_in.ap()],
                        outs=[ag_out.ap()]),
                        waits=[("dG", agw)], inc="tG")
                qhooks = {}
                if tc == NCH - 1:
                    def hk_carry():
                        cc_g["cc"] = emit_carry()
                    qhooks.setdefault(3, []).append(hk_carry)
                    for jj in range(8):
                        qhooks.setdefault(4 + jj, []).append(
                            (lambda j2: lambda: (emit_scan(2 * j2), emit_scan(2 * j2 + 1)))(jj))
                kv_pend = run_pipe(ws, q_stages, qhooks, pend=q_pend,
                                   next_aps=([strip(wkT, 0, 0, D), strip(wvT, 0, 0, D)]
                                             if tc + 1 < NCH else
                                             [strip(woT, 0, 0, D), strip(woT, 0, 128, D)]))

            ph_a.close()
            vh_h = sb(ph, "at_vhh", [128, NDT * TC], F16)
            vh_l = sb(ph, "at_vhl", [128, NDT * TC], F16)

            qsem = ("dR", "dR1")
            prev_vm = [P.counts["tV"], P.counts["tV"]]
            qr = {}
            wo_pend = kv_pend
            for tc in range(NCH):
                vh_guard = []

                def emit_qr(i_abs):
                    tcq, iq = divmod(i_abs, NDT)
                    qb = i_abs % 2
                    qr[i_abs] = P.emit("gpsimd",
                                       (lambda iq_, tcq_, qb_: lambda e: e.dma_start(
                                           out=q_sb[qb_][:, :],
                                           in_=hbm_tile_slice(q_page, iq_, tcq_ * TC, TC)))(iq, tcq, qb),
                                       waits=[("dQ", st_qpage["w"]), ("tV", prev_vm[qb])],
                                       inc=qsem[qb], inc_amt=16)
                if tc == 0:
                    emit_qr(0)
                    emit_qr(1)
                for i in range(NDT):
                    i_abs = tc * NDT + i
                    qb = i_abs % 2
                    vm = P.emit("vector",
                                (lambda i_, tc_, qb_: lambda e: e.tensor_tensor(
                                    out=kvt[:, i_ * NT + tc_ * TC: i_ * NT + (tc_ + 1) * TC],
                                    in0=q_sb[qb_][:, :],
                                    in1=kvt[:, i_ * NT + tc_ * TC: i_ * NT + (tc_ + 1) * TC],
                                    op=ALU.mult))(i, tc, qb),
                                waits=[(qsem[qb], qr[i_abs]),
                                       ("tV", scan_g[i])], inc="tV")
                    prev_vm[qb] = vm
                    if i_abs + 2 < 2 * NDT:
                        emit_qr(i_abs + 2)
                    vh1 = P.emit("scalar",
                                 (lambda i_, tc_: lambda e: e.activation(
                                     out=vh_h[:, i_ * TC:(i_ + 1) * TC],
                                     in_=kvt[:, i_ * NT + tc_ * TC: i_ * NT + (tc_ + 1) * TC],
                                     func=AF.Copy, bias=0.0, scale=1.0))(i, tc),
                                 waits=[("tV", vm), P.now("tP")], inc="tS")
                    vh2 = P.emit("vector",
                                 (lambda i_, tc_: lambda e: e.tensor_tensor(
                                     out=vh_l[:, i_ * TC:(i_ + 1) * TC],
                                     in0=kvt[:, i_ * NT + tc_ * TC: i_ * NT + (tc_ + 1) * TC],
                                     in1=vh_h[:, i_ * TC:(i_ + 1) * TC],
                                     op=ALU.subtract))(i, tc),
                                 waits=[("tS", vh1)], inc="tV")
                    vh_guard.append([("tS", vh1), ("tV", vh2)])

                def mk_oproj(f, tc_):
                    psid = f % 2
                    ps = (P0, P1)[psid]

                    def group(b, wready):
                        gdone = None
                        terms = ((0, 0), (0, 1), (1, 0))
                        for p, (wsel, asel) in enumerate(terms):
                            for k in range(NDT):
                                wts = []
                                if p == 0 and k == 0:
                                    wts = wready + [psg[psid]]
                                wts = wts + vh_guard[k]
                                last = (p == 2 and k == NDT - 1)
                                gdone = P.emit("tensor",
                                               (lambda b_, ps_, p_, k_, ws_, as_: lambda e: e.matmul(
                                                   ps_[:, 0:TC],
                                                   (whi, wlo)[ws_][b_][:, k_ * 128:(k_ + 1) * 128],
                                                   (vh_h, vh_l)[as_][:, k_ * TC:(k_ + 1) * TC],
                                                   start=(p_ == 0 and k_ == 0),
                                                   stop=(p_ == 2 and k_ == NDT - 1)))(
                                                       b, ps, p, k, wsel, asel),
                                               waits=wts,
                                               inc="tP" if last else None)
                        return gdone

                    def post(gdone):
                        radd = P.emit("vector",
                                      (lambda f_, tc2, ps_: lambda e: e.scalar_tensor_tensor(
                                          out=xst(f_, tc2 * TC, TC), in0=ps_[:, 0:TC], scalar=DENOM,
                                          in1=xst(f_, tc2 * TC, TC), op0=ALU.mult, op1=ALU.add))(
                                              f, tc_, ps),
                                      waits=[("tP", gdone)], inc="tV")
                        psg[psid] = ("tV", radd)
                        state["xg_h"][tc_][f] = [("tV", radd)]
                        if tc_ == NCH - 1:
                            state["xg"][f] = [("tV", radd)]
                    return (strip(woT, 0, f * 128, D), group, post)

                stages = [mk_oproj(f, tc) for f in range(NDT)]
                wo_pend = run_pipe(ws, stages, pend=wo_pend,
                                   next_aps=([strip(woT, 0, 0, D), strip(woT, 0, 128, D)]
                                             if tc + 1 < NCH else None))
            ph.close()

        emit_ffn("f1")
        emit_attn()
        emit_ffn("f2")

        yw = P.emit("gpsimd", lambda e: e.dma_start(out=hbm_tiles(y), in_=xs[:, :]),
                    waits=[g for gs in state["xg"] for g in gs], inc="dG", inc_amt=16)
        P.emit("gpsimd", lambda e: e.memset(s1row[0:1, 0:1], 0.0), waits=[("dG", yw)])

        with nc.Block() as block:
            @block.sync
            def _(eng):
                P.run("sync", eng)

            @block.scalar
            def _(eng):
                P.run("scalar", eng)

            @block.vector
            def _(eng):
                P.run("vector", eng)

            @block.tensor
            def _(eng):
                P.run("tensor", eng)

            @block.gpsimd
            def _(eng):
                P.run("gpsimd", eng)

    return nc


def make_in_maps(L0_attn_x, L0_attn_Wq, L0_attn_Wk, L0_attn_Wv, L0_attn_Wo,
                 L0_ffn_W0, L0_ffn_bias0, L0_ffn_W1, L0_ffn_bias1, n_cores=8):
    x = np.asarray(L0_attn_x, dtype=np.float32)
    wqT = np.ascontiguousarray(np.asarray(L0_attn_Wq, np.float32).T)
    wkT = np.ascontiguousarray(np.asarray(L0_attn_Wk, np.float32).T)
    wvT = np.ascontiguousarray(np.asarray(L0_attn_Wv, np.float32).T)
    woT = np.ascontiguousarray(np.asarray(L0_attn_Wo, np.float32).T)
    w0T = np.ascontiguousarray(np.asarray(L0_ffn_W0, np.float32).T)
    w1T = np.ascontiguousarray(np.asarray(L0_ffn_W1, np.float32).T)
    b0c = np.ascontiguousarray(np.asarray(L0_ffn_bias0, np.float32).reshape(NFT, 128).T)
    b1c = np.ascontiguousarray(np.asarray(L0_ffn_bias1, np.float32).reshape(NDT, 128).T)
    ident = np.eye(128, dtype=np.float32)
    in_maps = []
    for c in range(n_cores):
        b, h = c // 2, c % 2
        xTc = np.ascontiguousarray(x[b, h * NT:(h + 1) * NT, :].T)
        cs = np.zeros((8, 1), np.float32)
        if c % 2 == 1:
            cs[c - 1, 0] = 1.0
        in_maps.append({
            "xT": xTc, "wqT": wqT, "wkT": wkT, "wvT": wvT, "woT": woT,
            "w0T": w0T, "w1T": w1T, "b0c": b0c, "b1c": b1c,
            "csel": cs, "ident": ident,
        })
    return in_maps


def kernel(L0_attn_x, L0_attn_Wq, L0_attn_Wk, L0_attn_Wv, L0_attn_Wo,
           L0_ffn_W0, L0_ffn_bias0, L0_ffn_W1, L0_ffn_bias1):
    n_cores = 8
    nc = build_program(n_cores)
    in_maps = make_in_maps(L0_attn_x, L0_attn_Wq, L0_attn_Wk, L0_attn_Wv, L0_attn_Wo,
                           L0_ffn_W0, L0_ffn_bias0, L0_ffn_W1, L0_ffn_bias1, n_cores)
    res = run_bass_kernel_spmd(nc, in_maps, core_ids=list(range(n_cores)))
    out = np.empty((4, 2048, 2048), np.float32)
    for c in range(n_cores):
        b, h = c // 2, c % 2
        out[b, h * NT:(h + 1) * NT, :] = np.asarray(res.results[c]["y"]).T
    return out
